# version 32
# baseline (speedup 1.0000x reference)
import numpy as np
import ml_dtypes
import concourse.bass as bass
import concourse.mybir as mybir
from concourse.bass_utils import run_bass_kernel_spmd

F32 = mybir.dt.float32
BF16 = mybir.dt.bfloat16
FP8 = mybir.dt.float8e4
AF = mybir.ActivationFunctionType
ALU = mybir.AluOpType

D = 2048
T = 1024
M = 64
NG = 4
GL = 256
CH = 128
NT = 8
SB = 16
NIC = 48
TH = 512
SCALE = 0.125
CLAMP = 40.0
KB = 1024


class Prog:
    ENGINES = ("pe", "dve", "act", "pool", "sp")

    def __init__(self):
        self.ops = []
        self.events = {}
        self.ccount = {e: 0 for e in self.ENGINES}
        self.dch = {}
        self.dlast = {}

    def __init_coll(self):
        pass

    def op(self, eng, fn, waits=(), name=None, dma=False, coll=False, ch=None):
        for w in waits:
            if w not in self.events:
                raise KeyError(f"wait on undefined event {w!r} (op {name!r})")
        if name is not None:
            if name in self.events:
                raise KeyError(f"duplicate event {name!r}")
            if coll:
                idx = getattr(self, "ncoll", 0)
                self.ncoll = idx + 1
                self.events[name] = (("coll", idx), 1, eng)
            elif dma:
                k = ("d", eng, ch if ch is not None else "_")
                n = self.dch.get(k, 0) + 1
                self.dch[k] = n
                prev = self.dlast.get(k)
                if prev is not None and prev not in waits:
                    waits = tuple(waits) + (prev,)
                self.dlast[k] = name
                self.events[name] = (k, 16 * n, eng)
            else:
                self.ccount[eng] += 1
                self.events[name] = (("c", eng), self.ccount[eng], eng)
        self.ops.append(dict(eng=eng, fn=fn, waits=tuple(waits), name=name, dma=dma, coll=coll))

    def emit(self, eng_name, eng, sems, coll_sems):
        hw = {}
        for rec in self.ops:
            if rec["eng"] != eng_name:
                continue
            for w in rec["waits"]:
                key, val, peng = self.events[w]
                if key[0] == "coll":
                    if hw.get(key, -1) >= 1:
                        continue
                    eng.wait_ge(coll_sems[key[1]], 1)
                    hw[key] = 1
                    continue
                if key[0] == "c" and peng == eng_name:
                    continue
                if hw.get(key, -1) >= val:
                    continue
                eng.wait_ge(sems[key], val)
                hw[key] = val
            instr = rec["fn"](eng)
            if rec["name"] is not None:
                key, _, _ = self.events[rec["name"]]
                if key[0] == "coll":
                    instr.then_inc(coll_sems[key[1]])
                else:
                    instr.then_inc(sems[key], 16 if key[0] == "d" else 1)


_CACHE = {}


def build_nc():
    if "nc" in _CACHE:
        return _CACHE["nc"]
    nc = bass.Bass("TRN2", target_bir_lowering=False)

    def din(name, shape):
        return nc.dram_tensor(name, list(shape), F32, kind="ExternalInput").ap()

    x_main = din("x_main", (T, D))
    halo_fm = din("halo_fm", (128, 16, 3))
    w_inprojT = din("w_inprojT", (D, 128))
    w_memT = din("w_memT", (D, 128))
    w_ch0T = din("w_ch0T", (D, 128))
    w_ch1T = din("w_ch1T", (128, D))
    conv_w_fm = din("conv_w_fm", (128, 64))
    conv_b_fm = din("conv_b_fm", (128, 16))
    res_w_fm = din("res_w_fm", (128, 16))
    mem_b_col = din("mem_b_col", (128, 1))
    ch_b1_row = din("ch_b1_row", (1, D))
    dsc_row = din("dsc_row", (1, D))
    gsc_fm = din("gsc_fm", (128, 48))
    usc_fm = din("usc_fm", (128, 48))
    exmask = din("exmask", (M, 8))
    gate_sl = din("gate_sl", (D, 768))
    up_sl = din("up_sl", (D, 768))
    down_sl = din("down_sl", (768, D))

    out = nc.dram_tensor("out", [T, D], F32, kind="ExternalOutput").ap()

    ident_f = nc.inline_tensor(np.eye(128, dtype=np.float32), "ident_f").ap()
    trimask_np = np.tril(np.ones((128, 128), dtype=np.float32))
    trimask = nc.inline_tensor(trimask_np, "trimask").ap()

    def dint(name, shape, dt):
        return nc.dram_tensor(name, list(shape), dt, kind="Internal").ap()

    y1_sp = dint("y1_sp", (T, D), F32)
    x1_sp = dint("x1_sp", (T, D), F32)
    y2_sp = dint("y2_sp", (T, D), F32)
    cin_g = dint("cin_g", (128, 12288), BF16)
    cout_g = dint("cout_g", (1024, 12288), BF16)
    cin_u = dint("cin_u", (128, 12288), BF16)
    cout_u = dint("cout_u", (1024, 12288), BF16)
    cin_d = dint("cin_d", (768, D), BF16)
    cout_d = dint("cout_d", (NI := 6144, D), BF16)
    cin_x = dint("cin_x", (M + 2, D), BF16)
    cout_x = dint("cout_x", (8 * (M + 2), D), BF16)

    alloc = nc.alloc_sbuf_tensor_at
    cur = [18 * KB]

    def at(name, shape, dt, o=None):
        if o is None:
            o = cur[0]
        t = alloc(name, list(shape), dt, offset=o)
        nbytes = int(np.prod(shape[1:])) * {F32: 4, BF16: 2, FP8: 1}[dt]
        cur[0] = max(cur[0], o + ((nbytes + 31) // 32) * 32)
        return t

    ident_sb = at("ident_sb", (128, 128), F32)
    trimask_sb = at("trimask_sb", (128, 128), F32)
    ones_1x128_bf = at("ones_1x128_bf", (1, 128), BF16)
    ones_1x64_f = at("ones_1x64_f", (1, 64), F32)
    ones_c64 = at("ones_c64", (64, 1), F32)
    mem_b_sb = at("mem_b_sb", (128, 1), F32)
    conv_b_sb = at("conv_b_sb", (128, 16), F32)
    res_w_sb = at("res_w_sb", (128, 16), F32)
    conv_w_sb = at("conv_w_sb", (128, 64), F32)
    gsc_sb = at("gsc_sb", (128, 48), F32)
    usc_sb = at("usc_sb", (128, 48), F32)
    eps_col = at("eps_col", (128, 1), F32)
    exmask_sb = at("exmask_sb", (M, 8), F32)
    ch_b1_bf = at("ch_b1_bf", (1, D), BF16)
    dsc_bf = at("dsc_bf", (1, D), BF16)
    dsc_bcast = at("dsc_bcast", (128, D), BF16)
    ones_g = at("ones_g", (M, GL), F32)
    _bsb0 = cur[0]
    beta_sb = at("beta_sb", (M, 8), F32)
    beta_sb_bits = at("beta_sb_bits", (M, 16), BF16, _bsb0)
    _brn0 = cur[0]
    beta_run = at("beta_run", (M, 1), F32)
    beta_bits = at("beta_bits", (M, 2), BF16, _brn0)
    dfac_sb = at("dfac_sb", (M, 8), F32)
    rB_store = at("rB_store", (M, T), BF16)
    sum_parts = at("sum_parts", (128, NT, 4), F32)
    ssq_parts = at("ssq_parts", (128, NT, 4), F32)
    ln8_a = at("ln8_a", (128, 8), F32)
    ln8_b = at("ln8_b", (128, 8), F32)
    ln8_c = at("ln8_c", (128, 8), F32)

    S_f = at("S_f", (M, D), F32)
    S_bf = at("S_bf", (M, D), BF16)
    B_g = at("B_g", (M, GL), F32)
    r_s = at("r_s", (M, GL), F32)
    k_raw = at("k_raw", (M, GL), F32)
    kg_f = at("kg_f", (M, GL), F32)
    sel_b = at("sel_b", (M, GL), F32)
    tau_b = at("tau_b", (M, GL), F32)
    it_b = at("it_b", (M, GL), F32)
    gm_tmp = at("gm_tmp", (M, GL), F32)
    kn_row = at("kn_row", (1, GL), F32)
    t1_bf = at("t1_bf", (128, GL), BF16)
    eb_c = at("eb_c", (M, CH), F32)
    eB_c = at("eB_c", (M, CH), F32)
    arg_c = at("arg_c", (M, CH), F32)
    ek_c = at("ek_c", (M, SB), F32)
    negb = at("negb", (M, 1), F32)
    rb_bf = at("rb_bf", (M, CH), BF16)
    rhat_bf = at("rhat_bf", (M, CH), BF16)
    khat_bf = at("khat_bf", (M, SB), BF16)
    K2_f = at("K2_f", (M, CH), F32)
    K2T_bf = at("K2T_bf", (CH, M), BF16)
    PT_sb = at("PT_sb", (128, 128), F32)
    P_sb = at("P_sb", (128, 128), BF16)
    ident_bf = at("ident_bf", (128, 128), BF16)
    pkarg = [at(f"pkarg{i}", (M, 8, SB), F32) for i in range(2)]
    pargf = [at(f"pargf{i}", (M, 8, CH), F32) for i in range(2)]
    pkh = [at(f"pkh{i}", (M, 8, SB), BF16) for i in range(2)]
    prh = [at(f"prh{i}", (M, 8, CH), BF16) for i in range(2)]
    fence_sc = at("fence_sc", (128, 4), F32)
    fence_big = at("fence_big", (128, 512), F32)

    A0 = cur[0]
    y1b = [at(f"y1b{i}", (128, D), F32, A0 + i * 8 * KB) for i in range(2)]
    fuse = at("fuse", (128, NIC, TH), BF16, A0)
    diag_bf = at("diag_bf", (128, 16, 128), BF16, A0 + 16 * KB)
    cur[0] = A0 + 48 * KB

    C0 = cur[0]
    x_fm = at("x_fm", (128, 16, 272), BF16, C0)
    shift_fm = at("shift_fm", (128, 16, GL), BF16, C0 + 8704)
    u_bf = [at(f"u_bf{i}", (128, D), BF16, C0 + 16896 + i * 4 * KB) for i in range(2)]
    sres_tm = [at(f"sres{i}", (128, D), BF16, C0 + 25088 + i * 4 * KB) for i in range(2)]
    x_bf = [at(f"x_bf{i}", (128, D), BF16, C0 + 33280 + i * 4 * KB) for i in range(2)]
    convtmp = [at(f"convtmp{i}", (128, GL), F32, C0 + 41472 + i * KB) for i in range(2)]
    Zs_bf = [at(f"Zs{i}", (M, D), BF16, C0 + i * 4 * KB) for i in range(2)]
    T_state = at("T_state", (M, D), F32, C0 + 8 * KB)
    S_init_bf = at("S_init_bf", (M, D), BF16, C0 + 16 * KB)
    gst = [at(f"gst{i}", (128, 16, 256), BF16, C0 + 21 * KB + i * 8 * KB) for i in range(2)]
    ust = [at(f"ust{i}", (128, 16, 256), BF16, C0 + i * 8 * KB) for i in range(2)]
    sres_stage = at("sres_stage", (128, 16, 128), BF16, C0 + 43 * KB + 512)
    x1bf_t = at("x1bf_t", (128, D), BF16, C0 + 37 * KB)
    y2rl = [at(f"y2rl{i}", (128, D), F32, C0 + 21 * KB + i * 8 * KB) for i in range(2)]
    uscr = at("uscr", (128, TH), BF16, C0 + 41 * KB)
    sq_bf = at("sq_bf", (128, D), BF16, C0 + 43 * KB + 512)
    cur[0] = C0 + 48 * KB

    B0 = cur[0]
    q8_g = at("q8_g", (128, 8, 768), FP8, B0)
    qbf_g = at("qbf_g", (128, 8, 768), BF16, B0 + 6 * KB)
    q8_d = at("q8_d", (128, 3, D), FP8, B0)
    qbf_d = at("qbf_d", (128, 3, D), BF16, B0 + 6 * KB)
    x1_fm = at("x1_fm", (128, 16, T), BF16, B0)
    cur[0] = B0 + 32 * KB

    D0 = cur[0]
    w_ip_sb = at("w_ip_sb", (128, 16, 128), BF16, D0)
    w_mg_sb = at("w_mg_sb", (128, 16, 128), BF16, D0 + 4 * KB)
    w_c0_sb = at("w_c0_sb", (128, 16, 128), BF16, D0 + 8 * KB)
    w_c1_sb = at("w_c1_sb", (128, D), BF16, D0 + 12 * KB)
    dst_b = [at(f"dst{i}", (128, 4, 512), BF16, D0 + i * 4 * KB) for i in range(2)]
    x1rl = [at(f"x1rl{i}", (128, 512), F32, D0 + 8 * KB + i * 2 * KB) for i in range(2)]
    y2st = [at(f"y2st{i}", (128, 512), F32, D0 + 12 * KB + i * 2 * KB) for i in range(2)]
    cur[0] = D0 + 16 * KB
    assert cur[0] < 223 * KB, f"SBUF over budget: {cur[0]}"

    O_ps = nc.alloc_psum_tensor("O_ps", [128, 2048], F32)
    A_ps = nc.alloc_psum_tensor("A_ps", [128, 2048], F32)

    P = Prog()

    def fdve(nm, ap=None, p=64):
        P.op("dve", lambda e: e.tensor_copy(fence_big[:, 0:256], fence_big[:, 256:512]), name=nm)

    def fact(nm, ap=None, p=64):
        P.op("act", lambda e: e.activation(fence_big[:, 256:512], fence_big[:, 0:256], AF.Copy), name=nm)

    def gapD():
        P.op("dve", lambda e: e.tensor_copy(fence_big[:, 0:256], fence_big[:, 256:512]))

    def gapA():
        P.op("act", lambda e: e.activation(fence_big[:, 256:512], fence_big[:, 0:256], AF.Copy))

    P.op("sp", lambda e: e.dma_start(out=ident_sb[:], in_=ident_f), name="ld_ident", dma=True, ch="ld_ident")
    P.op("sp", lambda e: e.dma_start(out=trimask_sb[:], in_=trimask), name="ld_tri", dma=True, ch="ld_tri")
    for nm, dst, src in [("mb", mem_b_sb, mem_b_col), ("cb", conv_b_sb, conv_b_fm),
                         ("rw", res_w_sb, res_w_fm), ("cw", conv_w_sb, conv_w_fm),
                         ("gs", gsc_sb, gsc_fm), ("us", usc_sb, usc_fm),
                         ("xm", exmask_sb, exmask)]:
        P.op("sp", lambda e, d=dst, s=src: e.dma_start(out=d[:], in_=s), name=f"ld_{nm}", dma=True, ch=f"ld_{nm}")
    P.op("pool", lambda e: e.dma_start(out=x_fm[:, :, 13:16], in_=halo_fm), name="ld_halo", dma=True, ch="ld_halo")
    for nm, dst, src in [("ip", w_ip_sb, w_inprojT.rearrange("(a p) m -> p a m", p=128)),
                         ("mg", w_mg_sb, w_memT.rearrange("(a p) m -> p a m", p=128)),
                         ("c0", w_c0_sb, w_ch0T.rearrange("(a p) m -> p a m", p=128)),
                         ("c1", w_c1_sb, w_ch1T),
                         ("b1", ch_b1_bf, ch_b1_row),
                         ("dsc", dsc_bf, dsc_row)]:
        P.op("pool", lambda e, d=dst, s=src: e.dma_start(out=d[:], in_=s), name=f"cld_{nm}", dma=True, ch=f"cld_{nm}")
    for nm, buf, v in [("ones", ones_g, 1.0), ("o64", ones_1x64_f, 1.0), ("oc64", ones_c64, 1.0),
                       ("o128", ones_1x128_bf, 1.0), ("eps", eps_col, 1e-5), ("S", S_f, 0.0),
                       ("br", beta_run, 0.0)]:
        P.op("dve", lambda e, b=buf, v=v: e.memset(b[:], v), name=f"ms_{nm}")
    for s in range(4):
        P.op("pe", lambda e, s=s: e.matmul(A_ps[:, s * 512:(s + 1) * 512], ones_1x128_bf[:],
                                           dsc_bf[:, s * 512:(s + 1) * 512], start=True, stop=True),
             waits=["ms_o128", "cld_dsc"], name=f"mm_dscb{s}")
        P.op("dve", lambda e, s=s: e.tensor_copy(dsc_bcast[:, s * 512:(s + 1) * 512],
                                                 A_ps[:, s * 512:(s + 1) * 512]),
             waits=[f"mm_dscb{s}"], name=f"ev_dscb{s}")
    P.op("dve", lambda e: e.tensor_copy(ident_bf[:], ident_sb[:]), waits=["ld_ident"], name="ident_bf")
    for a in range(16):
        P.op("dve", lambda e, a=a: e.tensor_scalar(diag_bf[:, a, :], ident_sb[:, :],
                                                   res_w_sb[:, a:a + 1], None, ALU.mult),
             waits=(["ld_ident", "ld_rw"] if a == 0 else []), name=f"diag{a}")

    def xload(t):
        wy = [f"y1_spill{t - 2}"] if t >= 2 else []
        P.op("sp", lambda e, t=t: e.dma_start(out=y1b[t % 2][:], in_=x_main[t * 128:(t + 1) * 128, :]),
             waits=wy, name=f"xld{t}", dma=True, ch=f"xld{t % 2}")
        wb = [f"xbf_done{t - 2}"] if t >= 2 else []
        P.op("pool", lambda e, t=t: e.dma_start(out=x_bf[t % 2][:], in_=x_main[t * 128:(t + 1) * 128, :]),
             waits=wb, name=f"xbf{t}", dma=True, ch=f"xbf{t % 2}")

    xload(0)
    xload(1)

    def quant(tag, src_ap, cin_ap, q8, qbf):
        for ss in range(2):
            if tag == "d":
                src = src_ap[ss * 384:(ss + 1) * 384, :].rearrange("(c p) d -> p c d", p=128)
                cdst = cin_ap[ss * 384:(ss + 1) * 384, :].rearrange("(c p) d -> p c d", p=128)
            else:
                src = src_ap.rearrange("(a p) i -> p a i", p=128)[:, ss * 8:(ss + 1) * 8, :]
                cdst = cin_ap[:, ss * 6144:(ss + 1) * 6144].rearrange("p (a i) -> p a i", i=768)
            prev = {"g": ("ev_dscb3",), "u": ("qc_g1",), "d": ("ev_dscb3",)}[tag] if ss == 0 else (f"qc_{tag}0",)
            P.op("pool", lambda e, s=src, d=q8: e.dma_start(out=d[:, :, :], in_=s),
                 waits=list(prev), name=f"q8_{tag}{ss}", dma=True, ch="q8")
            P.op("dve", lambda e, d=qbf, s=q8: e.tensor_copy(d[:, :, :], s[:, :, :]),
                 waits=[f"q8_{tag}{ss}"], name=f"qb_{tag}{ss}")
            P.op("act", lambda e, c=cdst, d=qbf: e.dma_start(out=c, in_=d[:, :, :]),
                 waits=[f"qb_{tag}{ss}"], name=f"qc_{tag}{ss}", dma=True, ch="qc")

    def emit_ag(tag, ci, co):
        P.op("pool", lambda e, ci=ci, co=co: e.collective_compute(
            "AllGather", ALU.bypass, replica_groups=[list(range(8))],
            ins=[ci.opt()], outs=[co.opt()]),
            waits=[f"qc_{tag}1"], name=f"ag_{tag}", coll=True)

    quant("g", gate_sl, cin_g, q8_g, qbf_g)
    emit_ag("g", cin_g, cout_g)

    mixw = ["cld_ip", "cld_mg", "cld_c0", "cld_c1", "cld_b1", "ld_cw", "ld_cb", "ld_rw", "ld_mb"]

    for gi in range(NG):
        if gi == 1:
            quant("u", up_sl, cin_u, q8_g, qbf_g)
            emit_ag("u", cin_u, cout_u)
        if gi == 2:
            quant("d", down_sl, cin_d, q8_d, qbf_d)
            emit_ag("d", cin_d, cout_d)
        t0, t1 = 2 * gi, 2 * gi + 1
        prevgrp = [f"ch0mm{gi - 1}", f"conv_done{gi - 1}"] if gi > 0 else []
        if gi > 0:
            P.op("dve", lambda e: e.tensor_copy(x_fm[:, :, 13:16], x_fm[:, :, 269:272]),
                 waits=prevgrp, name=f"halo{gi}")
        halo_ev = f"halo{gi}" if gi > 0 else "ld_halo"
        for sl, t in enumerate((t0, t1)):
            for j in range(16):
                wtr = []
                if j == 0:
                    wtr = [f"xld{t}", "ld_ident"]
                    wtr += [f"xfme{t - 1}"] if sl == 1 else (
                        [f"y1acc{t0 - 1}"] if gi > 0 else [])
                P.op("pe", lambda e, j=j, t=t: e.transpose(
                    O_ps[:, j * 128:(j + 1) * 128], y1b[t % 2][:, j * 128:(j + 1) * 128],
                    ident_sb[:, :]), waits=wtr, name=(f"xtr{t}" if j == 15 else None))
            for q4 in range(4):
                we = [f"xtr{t}"]
                if q4 == 0:
                    we += prevgrp if sl == 0 else [halo_ev]
                P.op("act", lambda e, q4=q4, sl=sl: e.activation(
                    x_fm[:, 4 * q4:4 * q4 + 4, 16 + sl * 128:16 + (sl + 1) * 128],
                    O_ps[:, q4 * 512:(q4 + 1) * 512].rearrange("p (a n) -> p a n", n=128),
                    AF.Copy), waits=we, name=(f"xfme{t}" if q4 == 3 else None))
        for a in range(16):
            ct = convtmp[a % 2]
            wv = [f"xfme{t0}", f"xfme{t1}"] + (mixw if gi == 0 else [])
            if a >= 2 or gi > 0:
                wv += [f"silu{gi}_{a - 2}"] if a >= 2 else [f"silu{gi - 1}_{14 + a}"]
            P.op("dve", lambda e, a=a, ct=ct: e.tensor_scalar(
                ct[:, :], x_fm[:, a, 13:13 + GL], conv_w_sb[:, 4 * a:4 * a + 1], None, ALU.mult),
                waits=wv, name=f"cv{gi}_{a}_0")
            for i in range(1, 4):
                P.op("dve", lambda e, a=a, i=i, ct=ct: e.scalar_tensor_tensor(
                    ct[:, :], x_fm[:, a, 13 + i:13 + i + GL], conv_w_sb[:, 4 * a + i:4 * a + i + 1],
                    ct[:, :], ALU.mult, ALU.add), name=f"cv{gi}_{a}_{i}")
            wsilu = [f"cv{gi}_{a}_3"]
            if gi > 0:
                wsilu += [f"sresmm{t0 - 1}"]
            P.op("act", lambda e, a=a, ct=ct: e.activation(
                shift_fm[:, a, :], ct[:, :], AF.Silu, bias=conv_b_sb[:, a:a + 1]),
                waits=wsilu, name=f"silu{gi}_{a}")
        P.op("dve", lambda e: e.engine_nop(), waits=[f"cv{gi}_15_3"], name=f"conv_done{gi}")

        rk_ps = A_ps[:, 1536:1792]
        mg_ps = A_ps[:, 1024:1280]
        ch0_ps = A_ps[:, 1792:2048]
        wrk = [f"silu{gi}_{a}" for a in range(16)]
        if gi > 0:
            wrk += [f"rk_ev{gi - 1}", f"k_ev{gi - 1}", f"t1_ev{gi - 1}", f"sig{gi - 1}_1_2"]
        for a in range(16):
            P.op("pe", lambda e, a=a: e.matmul(rk_ps, w_ip_sb[:, a, :], shift_fm[:, a, :],
                                               start=(a == 0), stop=(a == 15)),
                 waits=(wrk if a == 0 else []), name=(f"rkmm{gi}" if a == 15 else None))
        wmg = [f"mg1_ev{gi - 1}", f"tau_ev{gi - 1}", f"sig{gi - 1}_1_3"] if gi > 0 else []
        for a in range(16):
            P.op("pe", lambda e, a=a: e.matmul(mg_ps, w_mg_sb[:, a, :], shift_fm[:, a, :],
                                               start=(a == 0), stop=(a == 15)),
                 waits=(wmg if a == 0 else []), name=(f"mgmm{gi}" if a == 15 else None))
        P.op("act", lambda e: e.activation(r_s[:, :], rk_ps[0:64, :], AF.Copy, scale=SCALE),
             waits=[f"rkmm{gi}"], name=f"rk_ev{gi}")
        P.op("act", lambda e: e.activation(k_raw[:, :], rk_ps[64:128, :], AF.Copy), name=f"k_ev{gi}")
        P.op("act", lambda e: e.activation(gm_tmp[:, :], mg_ps[0:64, :], AF.Identity, bias=mem_b_sb[0:64, :]),
             waits=[f"mgmm{gi}"], name=f"mg1_ev{gi}")
        P.op("act", lambda e: e.activation(tau_b[:, :], mg_ps[64:128, :], AF.Sigmoid,
                                           bias=mem_b_sb[64:128, :]), name=f"tau_ev{gi}")
        for a in range(16):
            w0 = [f"rk_ev{gi}", f"k_ev{gi}", f"xfme{t1}"] if a == 0 else []
            if a == 0 and gi > 0:
                w0 += [f"t1_ev{gi - 1}"]
            P.op("pe", lambda e, a=a: e.matmul(ch0_ps, w_c0_sb[:, a, :], x_fm[:, a, 16:16 + GL],
                                               start=(a == 0), stop=(a == 15)),
                 waits=w0, name=(f"ch0mm{gi}" if a == 15 else None))
        wt1 = [f"ch0mm{gi}"]
        if gi > 0:
            wt1 += [f"ch1mm{gi - 1}_1_3"]
        P.op("act", lambda e: e.activation(t1_bf[:, :], ch0_ps[:, :], AF.Copy), waits=wt1, name=f"t1_ev{gi}")

        P.op("dve", lambda e: e.tensor_scalar(sel_b[:, :], gm_tmp[:, :], 0.0, None, ALU.max),
             waits=[f"mg1_ev{gi}"], name=f"gmax{gi}")
        P.op("dve", lambda e: e.tensor_scalar(it_b[:, :], gm_tmp[:, :], -1.0, None, ALU.mult),
             name=f"gneg{gi}")
        gapD()
        P.op("dve", lambda e: e.tensor_max(it_b[:, :], it_b[:, :], gm_tmp[:, :]),
             name=f"gabs{gi}")
        P.op("act", lambda e: e.activation(it_b[:, :], it_b[:, :], AF.Exp, scale=-1.0),
             waits=[f"gabs{gi}"], name=f"gexp{gi}")
        P.op("dve", lambda e: e.tensor_scalar(it_b[:, :], it_b[:, :], 1.0, None, ALU.add),
             waits=[f"gexp{gi}"], name=f"gp1{gi}")
        P.op("act", lambda e: e.activation(it_b[:, :], it_b[:, :], AF.Ln),
             waits=[f"gp1{gi}"], name=f"gln{gi}")
        P.op("dve", lambda e: e.tensor_add(sel_b[:, :], sel_b[:, :], it_b[:, :]),
             waits=[f"gln{gi}", f"gmax{gi}"], name=f"sel{gi}")
        P.op("dve", lambda e: e.scalar_tensor_tensor(gm_tmp[:, :], sel_b[:, :], -1.0, tau_b[:, :],
                                                     ALU.mult, ALU.mult),
             waits=[f"sel{gi}", f"tau_ev{gi}"], name=f"g{gi}")
        gapD()
        wB = [f"g{gi}", "ms_ones"]
        if gi > 0:
            wB += [f"eB{gi - 1}_1", f"eb{gi - 1}_1", f"ek2{gi - 1}_done"]
        P.op("dve", lambda e: e.tensor_tensor_scan(B_g[:, :], ones_g[:, :], gm_tmp[:, :], 0.0,
                                                   ALU.mult, ALU.add), waits=wB, name=f"B{gi}")
        P.op("act", lambda e: e.activation(it_b[:, :], sel_b[:, :], AF.Ln), waits=[f"sel{gi}"],
             name=f"lnsel{gi}")
        P.op("dve", lambda e: e.tensor_mul(it_b[:, :], it_b[:, :], tau_b[:, :]),
             waits=[f"lnsel{gi}"], name=f"itm{gi}")
        P.op("act", lambda e: e.activation(it_b[:, :], it_b[:, :], AF.Exp), waits=[f"itm{gi}"],
             name=f"it{gi}")
        P.op("act", lambda e: e.activation(kg_f[:, :], k_raw[:, :], AF.Square), waits=[f"k_ev{gi}"],
             name=f"ksq{gi}")
        knorm_ps = A_ps[0:1, 1280:1536]
        kb_ps = A_ps[0:64, 1280:1536]
        wkn = [f"ksq{gi}", "ms_oc64"]
        P.op("pe", lambda e, knorm_ps=knorm_ps: e.matmul(knorm_ps, ones_c64[:, :], kg_f[:, :],
                                                         start=True, stop=True),
             waits=wkn, name=f"knmm{gi}")
        P.op("act", lambda e: e.activation(kn_row[0:1, :], knorm_ps, AF.Ln), waits=[f"knmm{gi}"],
             name=f"knln{gi}")
        gapA()
        P.op("act", lambda e: e.activation(kn_row[0:1, :], kn_row[0:1, :], AF.Exp, scale=-0.5),
             name=f"knexp{gi}_raw")
        fact(f"knexp{gi}", kn_row[0:1, 0:1], 1)
        P.op("dve", lambda e: e.tensor_scalar(kn_row[0:1, :], kn_row[0:1, :], 1e12, None, ALU.min),
             waits=[f"knexp{gi}"], name=f"knmin{gi}_raw")
        fdve(f"knmin{gi}", kn_row[0:1, 0:1], 1)
        P.op("pe", lambda e: e.matmul(kb_ps, ones_1x64_f[:], kn_row[0:1, :], start=True, stop=True),
             waits=[f"knmin{gi}", "ms_o64", f"knln{gi}"], name=f"knbc{gi}")
        P.op("dve", lambda e: e.tensor_mul(kg_f[:, :], k_raw[:, :], kb_ps), waits=[f"knbc{gi}"],
             name=f"kn{gi}")
        P.op("dve", lambda e: e.tensor_mul(kg_f[:, :], kg_f[:, :], it_b[:, :]),
             waits=[f"it{gi}", f"kn{gi}"], name=f"kg{gi}")

        for c in range(2):
            t = 2 * gi + c
            cs = c * CH
            ci = f"{gi}_{c}"
            ch1_main = A_ps[:, 1536:2048]
            ch1_alt = A_ps[:, 1024:1536]
            for s4 in range(4):
                ps = ch1_main if s4 % 2 == 0 else ch1_alt
                wch = []
                if s4 >= 2:
                    wch = [f"sig{ci}_{s4 - 2}"]
                if s4 == 1 and c == 0:
                    wch = [f"kn{gi}"]
                if s4 == 0:
                    wch = [f"t1_ev{gi}", "cld_c1", "cld_b1", "ms_o128"]
                    if t >= 2:
                        wch += [f"ulast{t - 2}"]
                    if t >= 1:
                        pt = t - 1
                        wch += [f"sig{pt // 2}_{pt % 2}_3"]
                    if c == 0:
                        wch += [f"mg1_ev{gi}", f"tau_ev{gi}"]
                P.op("pe", lambda e, s4=s4, ps=ps, c=c: e.matmul(
                    ps, t1_bf[:, c * 128:(c + 1) * 128], w_c1_sb[:, s4 * 512:(s4 + 1) * 512],
                    start=True, stop=False), waits=wch)
                P.op("pe", lambda e, s4=s4, ps=ps: e.matmul(
                    ps, ones_1x128_bf[:], ch_b1_bf[:, s4 * 512:(s4 + 1) * 512],
                    start=False, stop=True), name=f"ch1mm{ci}_{s4}")
                P.op("act", lambda e, s4=s4, ps=ps, t=t: e.activation(
                    u_bf[t % 2][:, s4 * 512:(s4 + 1) * 512], ps, AF.Sigmoid),
                    waits=[f"ch1mm{ci}_{s4}"], name=f"sig{ci}_{s4}")
            P.op("dve", lambda e, t=t: e.tensor_mul(u_bf[t % 2][:], u_bf[t % 2][:], x_bf[t % 2][:]),
                 waits=[f"sig{ci}_3", f"xbf{t}"], name=f"uprod{t}")
            P.op("dve", lambda e: e.engine_nop(), waits=[f"uprod{t}", f"xtr{t}"], name=f"xbf_done{t}")

            if c == 0:
                P.op("dve", lambda e: e.memset(negb[:], 0.0), waits=[f"B{gi}"], name=f"negb{ci}_raw")
            else:
                P.op("dve", lambda e, cs=cs: e.tensor_scalar(negb[:], B_g[:, cs - 1:cs], -1.0, None,
                                                             ALU.mult), waits=[f"B{gi}"], name=f"negb{ci}_raw")
            fdve(f"negb{ci}", negb[:, :])
            weB = [f"negb{ci}"]
            if t > 0:
                weB += [f"state{t - 1}"]
            P.op("act", lambda e, cs=cs: e.activation(eb_c[:, :], B_g[:, cs:cs + CH], AF.Exp,
                                                      bias=negb[:, :]), waits=weB, name=f"eb{ci}")
            P.op("dve", lambda e, cs=cs: e.tensor_mul(rb_bf[:, :], r_s[:, cs:cs + CH], eb_c[:, :]),
                 waits=[f"eb{ci}", f"rk_ev{gi}"], name=f"rb{ci}")
            wsb = [f"state{t - 1}"] if t > 0 else ["ms_S"]
            P.op("act", lambda e: e.activation(S_bf[:, :], S_f[:, :], AF.Copy), waits=wsb, name=f"sbf{ci}")
            for s4 in range(4):
                wo = []
                if s4 == 0:
                    wo = [f"rb{ci}", f"sbf{ci}", f"xfme{2 * gi + 1}"]
                    if t >= 1:
                        wo += [f"y1acc{t - 1}"]
                P.op("pe", lambda e, s4=s4: e.matmul(
                    O_ps[:, s4 * 512:(s4 + 1) * 512], rb_bf[:, :], S_bf[:, s4 * 512:(s4 + 1) * 512],
                    start=True, stop=False), waits=wo, name=(f"ointer{ci}" if s4 == 3 else None))
            for a in range(16):
                P.op("pe", lambda e, a=a, c=c: e.matmul(
                    O_ps[:, a * 128:(a + 1) * 128], shift_fm[:, a, c * 128:(c + 1) * 128],
                    diag_bf[:, a, :], start=False, stop=False),
                    waits=(["diag15"] if a == 0 else []), name=(f"sresmm{t}" if a == 15 else None))
            PT_ps = A_ps[:, 0:128]
            pb = t % 2
            wprep = [f"B{gi}"]
            if t >= 2:
                wprep += [f"ptmm{gi - 1}_{c}"]
            for I in range(8):
                ie = cs + I * SB + SB - 1
                P.op("dve", lambda e, I=I, cs=cs, ie=ie, pb=pb: e.tensor_scalar(
                    pkarg[pb][:, I, :], B_g[:, cs + I * SB:cs + I * SB + SB],
                    B_g[:, ie:ie + 1], None, ALU.subtract),
                    waits=(wprep if I == 0 else []), name=f"eka{ci}_{I}")
                P.op("dve", lambda e, I=I, cs=cs, ie=ie, pb=pb: e.tensor_scalar(
                    pargf[pb][:, I, :], B_g[:, cs:cs + CH], B_g[:, ie:ie + 1], CLAMP,
                    ALU.subtract, ALU.min), name=f"parg{ci}_{I}")
            fdve(f"pprep{ci}")
            P.op("act", lambda e, pb=pb: e.activation(pkarg[pb][:, :, :], pkarg[pb][:, :, :],
                                                      AF.Exp, scale=-1.0),
                 waits=[f"pprep{ci}"], name=f"ekx{ci}")
            P.op("act", lambda e, pb=pb: e.activation(pargf[pb][:, :, :], pargf[pb][:, :, :],
                                                      AF.Exp), name=f"erhx{ci}")
            for I in range(8):
                P.op("dve", lambda e, I=I, cs=cs, pb=pb: e.tensor_mul(
                    pkh[pb][:, I, :], kg_f[:, cs + I * SB:cs + I * SB + SB], pkarg[pb][:, I, :]),
                    waits=([f"ekx{ci}", f"kg{gi}"] if I == 0 else []), name=f"pkh{ci}_{I}")
                P.op("dve", lambda e, I=I, cs=cs, pb=pb: e.tensor_mul(
                    prh[pb][:, I, :], r_s[:, cs:cs + CH], pargf[pb][:, I, :]),
                    waits=([f"erhx{ci}"] if I == 0 else []), name=f"prh{ci}_{I}")
            fdve(f"prdy{ci}")
            for I in range(8):
                wpt = [f"prdy{ci}"] if I == 0 else []
                if I == 0 and t > 0:
                    wpt += [f"zdone{t - 1}"]
                P.op("pe", lambda e, I=I, pb=pb: e.matmul(
                    PT_ps[:, I * SB:(I + 1) * SB], prh[pb][:, I, :], pkh[pb][:, I, :],
                    start=True, stop=True),
                    waits=wpt, name=(f"ptmm{ci}" if I == 7 else None))
            P.op("dve", lambda e: e.tensor_mul(PT_sb[:, :], PT_ps, trimask_sb[:, :]),
                 waits=[f"ptmm{ci}", "ld_tri"], name=f"ptev{ci}")
            P2_ps = A_ps[:, 512:640]
            wp2 = [f"ptev{ci}", "ld_ident"]
            if t > 0:
                wp2 += [f"zdone{t - 1}"]
            if c == 0:
                wp2 += [f"knln{gi}", f"kn{gi}"]
            P.op("pe", lambda e: e.transpose(P2_ps, PT_sb[:, :], ident_sb[:, :]), waits=wp2,
                 name=f"p2{ci}")
            P.op("dve", lambda e: e.tensor_copy(P_sb[:, :], P2_ps), waits=[f"p2{ci}"], name=f"pev{ci}")
            for s4 in range(4):
                wintra = [f"pev{ci}", f"uprod{t}", f"ointer{ci}"] if s4 == 0 else []
                P.op("pe", lambda e, s4=s4, t=t: e.matmul(
                    O_ps[:, s4 * 512:(s4 + 1) * 512], P_sb[:, :], u_bf[t % 2][:, s4 * 512:(s4 + 1) * 512],
                    start=False, stop=True), waits=wintra, name=(f"ointra{ci}" if s4 == 3 else None))
            P.op("act", lambda e, cs=cs: e.activation(eB_c[:, :], B_g[:, cs:cs + CH], AF.Exp,
                                                      bias=beta_run[:, :]),
                 waits=[f"B{gi}", "ms_br"] + ([f"brun{gi - 1}"] if gi > 0 else []), name=f"eB{ci}")
            P.op("dve", lambda e, cs=cs, t=t: e.tensor_mul(rB_store[:, t * 128:(t + 1) * 128],
                                                           r_s[:, cs:cs + CH], eB_c[:, :]),
                 waits=[f"eB{ci}"], name=f"rB{ci}")
            P.op("dve", lambda e, t=t: e.scalar_tensor_tensor(
                y1b[t % 2][:], O_ps[:, :], 1.0, y1b[t % 2][:], ALU.mult, ALU.add),
                waits=[f"ointra{ci}", f"sresmm{t}", f"xld{t}"], name=f"y1acc{t}")
            P.op("pool", lambda e, t=t: e.dma_start(out=y1_sp[t * 128:(t + 1) * 128, :], in_=y1b[t % 2][:]),
                 waits=[f"y1acc{t}"], name=f"y1_spill{t}", dma=True, ch=f"y1sp{t % 2}")
            wek2 = [f"B{gi}"]
            if t > 0:
                wek2 += [f"state{t - 1}"]
            P.op("act", lambda e, cs=cs: e.activation(K2_f[:, :], B_g[:, cs:cs + CH], AF.Exp, scale=-1.0,
                                                      bias=B_g[:, cs + CH - 1:cs + CH]),
                 waits=wek2, name=f"ek2{ci}")
            P.op("dve", lambda e, cs=cs: e.tensor_mul(K2_f[:, :], K2_f[:, :], kg_f[:, cs:cs + CH]),
                 waits=[f"ek2{ci}", f"kg{gi}"], name=f"K2{ci}")
            if c == 1:
                P.op("dve", lambda e: e.engine_nop(), waits=[f"K2{ci}"], name=f"ek2{gi}_done")
            K2T_ps = A_ps[:, 128:192]
            wk2t = [f"K2{ci}", f"ptev{ci}", "ld_ident"]
            if t > 0:
                wk2t += [f"zdone{t - 1}"]
            P.op("pe", lambda e: e.transpose(K2T_ps, K2_f[:, :], ident_sb[0:64, 0:64]), waits=wk2t,
                 name=f"k2t{ci}")
            P.op("dve", lambda e: e.tensor_copy(K2T_bf[:, :], K2T_ps), waits=[f"k2t{ci}"],
                 name=f"k2ev{ci}")
            for p4 in range(4):
                Z_ps = A_ps[0:64, (p4 % 2) * 512:(p4 % 2) * 512 + 512]
                wz = []
                if p4 == 0:
                    wz = [f"k2ev{ci}", f"uprod{t}", f"ptev{ci}"]
                if p4 == 1:
                    wz = [f"pev{ci}"]
                    if c == 0:
                        wz += [f"kn{gi}"]
                if p4 >= 2:
                    wz = [f"st{ci}_{p4 - 2}"]
                P.op("pe", lambda e, Z=Z_ps, t=t, p4=p4: e.matmul(
                    Z, K2T_bf[:, :], u_bf[t % 2][:, p4 * 512:(p4 + 1) * 512], start=True, stop=True),
                    waits=wz, name=f"zmm{ci}_{p4}")
                wst = [f"zmm{ci}_{p4}", f"eb{ci}"]
                if t > 0 and p4 == 0:
                    wst += [f"sbf{ci}"]
                P.op("dve", lambda e, Z=Z_ps, p4=p4, cs=cs: e.scalar_tensor_tensor(
                    S_f[:, p4 * 512:(p4 + 1) * 512], S_f[:, p4 * 512:(p4 + 1) * 512],
                    eb_c[:, CH - 1:CH], Z, ALU.mult, ALU.add), waits=wst, name=f"st{ci}_{p4}")
            P.op("dve", lambda e: e.engine_nop(), waits=[f"st{ci}_3"], name=f"state{t}")
            P.op("dve", lambda e: e.engine_nop(), waits=[f"st{ci}_3"], name=f"zdone{t}")
            P.op("dve", lambda e: e.engine_nop(), waits=[f"zmm{ci}_3"], name=f"ulast{t}")
            if t + 2 < NT:
                xload(t + 2)
        P.op("dve", lambda e: e.tensor_add(beta_run[:, :], beta_run[:, :], B_g[:, GL - 1:GL]),
             waits=[f"eB{gi}_0", f"eB{gi}_1", f"B{gi}"], name=f"brun{gi}_raw")
        fdve(f"brun{gi}", beta_run[:, :])

    P.op("dve", lambda e: e.memset(T_state[:], 0.0), waits=["sresmm7"], name="ms_T")
    P.op("act", lambda e: e.activation(S_bf[:, :], S_f[:, :], AF.Copy),
         waits=["state7", "ointer3_1"], name="zseg")
    P.op("pool", lambda e: e.dma_start(out=cin_x[0:M, :], in_=S_bf[:, :]), waits=["zseg"], name="zst",
         dma=True, ch="zst")
    P.op("pool", lambda e: e.dma_start(
        out=cin_x[M:M + 1, 0:2 * M].rearrange("o (m b) -> m (o b)", m=M), in_=beta_bits[:, :]),
        waits=["brun3"], name="bst", dma=True, ch="bst")
    P.op("pool", lambda e: e.collective_compute("AllGather", ALU.bypass, replica_groups=[list(range(8))],
                                                ins=[cin_x.opt()], outs=[cout_x.opt()]),
         waits=["zst", "bst"], name="ag_s", coll=True)
    P.op("sp", lambda e: e.dma_start(
        out=beta_sb_bits.ap().rearrange("m (s b) -> m s b", b=2),
        in_=cout_x.rearrange("(s r) c -> s r c", r=M + 2)[:, M, 0:2 * M].rearrange(
            "s (m b) -> m s b", m=M),
        allow_slow_non_contiguous=True),
        waits=["ag_s"], name="bld", dma=True, ch="bld")
    P.op("dve", lambda e: e.tensor_mul(dfac_sb[:, :], beta_sb[:, :], exmask_sb[:, :]),
         waits=["bld", "ld_xm"], name="dfm_raw")
    fdve("dfm", dfac_sb[:, 7:8])
    P.op("act", lambda e: e.activation(dfac_sb[:, :], dfac_sb[:, :], AF.Exp), waits=["dfm"], name="dfac_raw")
    fact("dfac", dfac_sb[:, 7:8])
    for s in range(8):
        wz = ["ag_s"]
        if s >= 2:
            wz += [f"exstep{s - 2}"]
        else:
            wz += ["ch0mm3", "conv_done3"]
        P.op("sp", lambda e, s=s: e.dma_start(out=Zs_bf[s % 2][:, :],
                                              in_=cout_x[s * (M + 2):s * (M + 2) + M, :]),
             waits=wz, name=f"zld{s}", dma=True, ch=f"zld{s % 2}")
        P.op("dve", lambda e, s=s: e.tensor_scalar(T_state[:, :], T_state[:, :], dfac_sb[:, s:s + 1],
                                                   None, ALU.mult),
             waits=["dfac", "ms_T"] + ([f"exstep{s - 1}"] if s > 0 else []), name=f"exd{s}")
        P.op("dve", lambda e, s=s: e.scalar_tensor_tensor(
            T_state[:, :], Zs_bf[s % 2][:, :], exmask_sb[:, s:s + 1], T_state[:, :], ALU.mult, ALU.add),
            waits=[f"zld{s}", f"exd{s}"], name=f"exstep{s}")
    P.op("act", lambda e: e.activation(S_init_bf[:, :], T_state[:, :], AF.Copy),
         waits=["exstep7", "ulast6"], name="sinit")

    def emit_x1t(t):
        pps = [O_ps, A_ps][t % 2]
        for j in range(16):
            wtr = []
            if j == 0:
                wtr = [f"x1b{t}", f"p2acc{t}"]
            P.op("pe", lambda e, j=j, t=t, pps=pps: e.transpose(
                pps[:, j * 128:(j + 1) * 128], y1b[t % 2][:, j * 128:(j + 1) * 128],
                ident_sb[:, :]), waits=wtr, name=(f"xt1_{t}" if j == 15 else None))
        for q4 in range(4):
            we = [f"xt1_{t}"]
            if q4 == 0 and t == 0:
                we += ["qc_d1"]
            P.op("act", lambda e, q4=q4, t=t, pps=pps: e.activation(
                x1_fm[:, 4 * q4:4 * q4 + 4, t * 128:(t + 1) * 128],
                pps[:, q4 * 512:(q4 + 1) * 512].rearrange("p (a n) -> p a n", n=128),
                AF.Copy), waits=we, name=(f"x1fm{t}" if q4 == 3 else None))

    for t in range(NT):
        wrl = [f"y1_spill{t}"]
        wrl += [f"y1_spill{6 + (t % 2)}"]
        if t >= 2:
            wrl += [f"x1sp{t - 2}", f"x1fm{t - 2}"]
        P.op("sp", lambda e, t=t: e.dma_start(out=y1b[t % 2][:], in_=y1_sp[t * 128:(t + 1) * 128, :]),
             waits=wrl, name=f"y1rl{t}", dma=True, ch=f"xld{t % 2}")
        pps = [O_ps, A_ps][t % 2]
        for s4 in range(4):
            wc = []
            if s4 == 0:
                wc = ["sinit"]
                wc += [f"p2acc{t - 2}", f"x1fm{t - 2}"] if t >= 2 else (["y1acc7"] if t == 0 else [])
            P.op("pe", lambda e, s4=s4, t=t, pps=pps: e.matmul(
                pps[:, s4 * 512:(s4 + 1) * 512], rB_store[:, t * 128:(t + 1) * 128],
                S_init_bf[:, s4 * 512:(s4 + 1) * 512], start=True, stop=True),
                waits=wc, name=(f"cor{t}" if s4 == 3 else None))
        if t > 0:
            emit_x1t(t - 1)
        P.op("dve", lambda e, t=t, pps=pps: e.scalar_tensor_tensor(
            y1b[t % 2][:], pps[:, :], 1.0, y1b[t % 2][:], ALU.mult, ALU.add),
            waits=[f"cor{t}", f"y1rl{t}"], name=f"p2acc{t}")
        P.op("dve", lambda e, t=t: e.reduce_sum(sum_parts[:, t, 0:1], y1b[t % 2][:],
                                                axis=mybir.AxisListType.X), name=f"p2rs{t}")
        wsq = [f"p2acc{t}"]
        P.op("dve", lambda e, t=t: e.scalar_tensor_tensor(
            sq_bf[:], y1b[t % 2][:], 1.0, y1b[t % 2][:], ALU.mult, ALU.mult),
            waits=wsq, name=f"p2sq{t}_sq")
        P.op("dve", lambda e, t=t: e.reduce_sum(ssq_parts[:, t, 0:1], sq_bf[:],
                                                axis=mybir.AxisListType.X), name=f"p2sq{t}")
        gapD()
        P.op("dve", lambda e, t=t: e.tensor_scalar(ln8_a[:, t:t + 1], sum_parts[:, t, 0:1],
                                                   1.0 / D, None, ALU.mult),
             waits=[f"p2acc{t}"], name=f"p2m{t}")
        gapD()
        P.op("dve", lambda e, t=t: e.tensor_mul(ln8_b[:, t:t + 1], ln8_a[:, t:t + 1], ln8_a[:, t:t + 1]),
             name=f"p2m2{t}")
        gapD()
        P.op("dve", lambda e, t=t: e.scalar_tensor_tensor(
            ln8_c[:, t:t + 1], ssq_parts[:, t, 0:1], 1.0 / D, ln8_b[:, t:t + 1],
            ALU.mult, ALU.subtract), waits=[f"p2sq{t}", f"p2m2{t}"], name=f"p2var{t}_raw")
        fdve(f"p2var{t}", ln8_c[:, t:t + 1], 128)
        P.op("act", lambda e, t=t: e.activation(ln8_c[:, t:t + 1], ln8_c[:, t:t + 1], AF.Ln,
                                                bias=eps_col[:, :]),
             waits=[f"p2var{t}", "ms_eps"], name=f"p2ln{t}")
        gapA()
        P.op("act", lambda e, t=t: e.activation(ln8_c[:, t:t + 1], ln8_c[:, t:t + 1], AF.Exp, scale=-0.5),
             name=f"p2rstd{t}_raw")
        fact(f"p2rstd{t}", ln8_c[:, t:t + 1], 128)
        P.op("dve", lambda e, t=t: e.tensor_scalar(y1b[t % 2][:], y1b[t % 2][:], ln8_a[:, t:t + 1],
                                                   None, ALU.subtract),
             waits=[f"p2rstd{t}", f"p2sq{t}"], name=f"x1a{t}")
        P.op("dve", lambda e, t=t: e.tensor_scalar(y1b[t % 2][:], y1b[t % 2][:], ln8_c[:, t:t + 1],
                                                   None, ALU.mult), name=f"x1b{t}")
        P.op("pool", lambda e, t=t: e.dma_start(out=x1_sp[t * 128:(t + 1) * 128, :], in_=y1b[t % 2][:]),
             waits=[f"x1b{t}"], name=f"x1sp{t}", dma=True, ch=f"x1sp{t % 2}")
    emit_x1t(NT - 1)

    gbank = [O_ps[:, 0:512], O_ps[:, 1024:1536]]
    ubank = [O_ps[:, 512:1024], O_ps[:, 1536:2048]]

    def emit_ln2_stats(b):
        c4 = slice(4 * b, 4 * b + 4)
        w = [f"y2q{b}_3_{x}" for x in range(4)]
        P.op("dve", lambda e: e.reduce_sum(ln8_a[:, c4], sum_parts[:, c4, :],
                                           axis=mybir.AxisListType.X), waits=w, name=f"fsum{b}")
        gapD()
        P.op("dve", lambda e: e.reduce_sum(ln8_b[:, c4], ssq_parts[:, c4, :],
                                           axis=mybir.AxisListType.X), name=f"fssq{b}")
        P.op("dve", lambda e: e.tensor_scalar(ln8_a[:, c4], ln8_a[:, c4], 1.0 / D, None, ALU.mult),
             waits=[f"fsum{b}"], name=f"fm{b}")
        gapD()
        P.op("dve", lambda e: e.tensor_mul(ln8_c[:, c4], ln8_a[:, c4], ln8_a[:, c4]), name=f"fm2{b}")
        gapD()
        P.op("dve", lambda e: e.scalar_tensor_tensor(
            ln8_b[:, c4], ln8_b[:, c4], 1.0 / D, ln8_c[:, c4], ALU.mult, ALU.subtract),
            waits=[f"fssq{b}", f"fm2{b}"], name=f"fvar{b}_raw")
        fdve(f"fvar{b}")
        P.op("act", lambda e: e.activation(ln8_b[:, c4], ln8_b[:, c4], AF.Ln, bias=eps_col[:, :]),
             waits=[f"fvar{b}"], name=f"fln{b}")
        gapA()
        P.op("act", lambda e: e.activation(ln8_b[:, c4], ln8_b[:, c4], AF.Exp, scale=-0.5),
             name=f"frstd{b}_raw")
        fact(f"frstd{b}")

    def emit_y2r(tg):
        wrl = [f"y2w{tg // 4}_3_{tg % 4}", "umm1_47", "gmm1_47"]
        if tg >= 2:
            wrl += [f"outw{tg - 2}"]
        P.op("act", lambda e, tg=tg: e.dma_start(out=y2rl[tg % 2][:], in_=y2_sp[tg * 128:(tg + 1) * 128, :]),
             waits=wrl, name=f"y2r{tg}", dma=True, ch=f"y2r{tg % 2}")

    def emit_ln2_tile(tg):
        P.op("dve", lambda e, tg=tg: e.tensor_scalar(y2rl[tg % 2][:], y2rl[tg % 2][:],
                                                     ln8_a[:, tg:tg + 1], None, ALU.subtract),
             waits=[f"frstd{tg // 4}", f"y2r{tg}"], name=f"outa{tg}")
        P.op("dve", lambda e, tg=tg: e.tensor_scalar(y2rl[tg % 2][:], y2rl[tg % 2][:],
                                                     ln8_b[:, tg:tg + 1], None, ALU.mult),
             name=f"outb{tg}")
        eng = "act" if tg < 4 else "sp"
        P.op(eng, lambda e, tg=tg: e.dma_start(out=out[tg * 128:(tg + 1) * 128, :], in_=y2rl[tg % 2][:]),
             waits=[f"outb{tg}"], name=f"outw{tg}", dma=True, ch=f"outw{eng}{tg % 2}")

    for h in range(2):
        def emit_gu_load(h, idx):
            r, q = divmod(idx, 3)
            for tag, co, buf in [("g", cout_g, gst), ("u", cout_u, ust)]:
                src = co[r * 128:(r + 1) * 128, :].rearrange("p (a i) -> p a i", i=768)[
                    :, :, q * 256:(q + 1) * 256]
                wld = [f"ag_{tag}"]
                if h == 0 and idx in (0, 1):
                    wld += ["ulast6", "ulast7", "xbf_done6", "xbf_done7"]
                if idx >= 2 or h > 0:
                    pidx, ph = (idx - 2, h) if idx >= 2 else (22 + idx, h - 1)
                    wld += [f"{'gmm' if tag == 'g' else 'umm'}{ph}_{2 * pidx + 1}"]
                if tag == "u" and h == 0 and idx in (0, 1):
                    wld += ["exstep7" if idx == 0 else "sinit"]
                P.op("sp", lambda e, s=src, b=buf, idx=idx: e.dma_start(out=b[idx % 2][:, :, :], in_=s),
                     waits=wld, name=f"{tag}ld{h}_{idx}", dma=True, ch=f"{tag}ld{idx % 2}")

        emit_gu_load(h, 0)
        emit_gu_load(h, 1)
        for qq in range(NIC):
            idx, ic2 = divmod(qq, 2)
            sl48 = (qq + 16) % NIC
            for tag, buf, bank, nm in [("g", gst, gbank[qq % 2], "gmm"), ("u", ust, ubank[qq % 2], "umm")]:
                for a in range(16):
                    wmmq = []
                    if a == 0:
                        wmmq = [f"{tag}ld{h}_{idx}", f"x1fm{4 * h + 3}"]
                        if qq >= 2:
                            wmmq += [f"{'gev' if tag == 'g' else 'uev'}{h}_{qq - 2}"]
                        elif h == 0:
                            wmmq += ["p2acc6", "y1acc7", "x1fm6"]
                        else:
                            wmmq += [f"y2m0_3_{x}" for x in range(4)]
                    P.op("pe", lambda e, a=a, b=buf, idx=idx, ic2=ic2, bank=bank, h=h: e.matmul(
                        bank, b[idx % 2][:, a, ic2 * 128:(ic2 + 1) * 128],
                        x1_fm[:, a, h * 512:(h + 1) * 512], start=(a == 0), stop=(a == 15)),
                        waits=wmmq, name=(f"{nm}{h}_{qq}" if a == 15 else None))
            wge = [f"gmm{h}_{qq}", "ld_gs"]
            if h == 0 and qq == 0:
                wge += ["sresmm7"]
            if h == 0 and qq == 32:
                wge += ["x1sp7", "x1fm7"]
            if h == 1:
                wge += ["dfin0_3"]
            P.op("act", lambda e, qq=qq, sl=sl48, bank=gbank[qq % 2]: e.activation(
                fuse[:, sl, :], bank, AF.Silu, scale=gsc_sb[:, qq:qq + 1]),
                waits=wge, name=f"gev{h}_{qq}")
            P.op("dve", lambda e, qq=qq, bank=ubank[qq % 2]: e.tensor_scalar(
                uscr[:, :], bank, usc_sb[:, qq:qq + 1], None, ALU.mult),
                waits=[f"umm{h}_{qq}", "ld_us"] + ([f"fuse{h}_{qq - 1}"] if qq > 0 else
                                                  ([f"fuse{h - 1}_47"] if h > 0 else
                                                   ["conv_done3", "silu3_15"])),
                name=f"uev{h}_{qq}")
            P.op("dve", lambda e, qq=qq, sl=sl48: e.tensor_mul(fuse[:, sl, :], fuse[:, sl, :], uscr[:, :]),
                 waits=[f"gev{h}_{qq}", f"uev{h}_{qq}"], name=f"fuse{h}_{qq}")
            if qq % 2 == 1 and qq // 2 + 2 < 24:
                emit_gu_load(h, qq // 2 + 2)
        def emit_d_load(h, s, ig):
            wd = ["ag_d"]
            if h == 0 and s == 0 and ig in (0, 1):
                wd += ["rkmm3", "mgmm3"]
            if ig >= 2:
                wd += [f"dcons{h}_{s}_{ig - 2}"]
            elif s > 0:
                wd += [f"dcons{h}_{s - 1}_{10 + ig}"]
            elif h > 0:
                wd += [f"dcons{h - 1}_3_{10 + ig}"]
            P.op("sp", lambda e, s=s, ig=ig: e.dma_start(
                out=dst_b[ig % 2][:, :, :],
                in_=cout_d[ig * 512:(ig + 1) * 512, s * 512:(s + 1) * 512].rearrange(
                    "(c p) n -> p c n", p=128)),
                waits=wd, name=f"dld{h}_{s}_{ig}", dma=True, ch=f"dld{ig % 2}")

        for s in range(4):
            dbank = [A_ps, O_ps][s % 2]
            emit_d_load(h, s, 0)
            for ig in range(12):
                for ic in range(4):
                    q = ig * 4 + ic
                    for tt in range(4):
                        wdm = []
                        if ic == 0 and tt == 0:
                            wdm = [f"dld{h}_{s}_{ig}"]
                        if q == 0 and tt == 0:
                            wdm += [f"fuse{h}_47"]
                            if s == 0:
                                wdm += ["p2acc7", "x1fm7"] if h == 0 else [f"y2m0_2_{x}" for x in range(4)]
                            elif s == 1:
                                wdm += [f"gev{h}_46", f"gev{h}_47", f"uev{h}_46", f"uev{h}_47"]
                            else:
                                wdm += [f"y2m{h}_{s - 2}_{x}" for x in range(4)]
                        nm = None
                        if ic == 3 and tt == 3:
                            nm = f"dcons{h}_{s}_{ig}"
                        P.op("pe", lambda e, q=q, tt=tt, ig=ig, ic=ic, s=s, h=h, dbank=dbank: e.matmul(
                            dbank[:, tt * 512:(tt + 1) * 512], fuse[:, (q + 16) % NIC, tt * 128:(tt + 1) * 128],
                            dst_b[ig % 2][:, ic, :],
                            start=(q == 0), stop=(q == NIC - 1)), waits=wdm, name=nm)
                if ig + 1 < 12:
                    emit_d_load(h, s, ig + 1)
            P.op("dve", lambda e: e.engine_nop(), waits=[f"dcons{h}_{s}_11"], name=f"dfin{h}_{s}")
            for tt in range(4):
                tg = h * 4 + tt
                wx = [f"x1sp{tg}"]
                if h == 0 and s == 0 and tt in (0, 1):
                    wx += ["ch0mm3"]
                if (s * 4 + tt) >= 2:
                    px = (s, tt - 2) if tt >= 2 else (s - 1, tt + 2)
                    wx += [f"y2a{h}_{px[0]}_{px[1]}"]
                elif h > 0:
                    px = (3, tt + 2)
                    wx += [f"y2a{h - 1}_{px[0]}_{px[1]}"]
                P.op("pool", lambda e, s=s, tg=tg: e.dma_start(
                    out=x1rl[tg % 2][:, :], in_=x1_sp[tg * 128:(tg + 1) * 128, s * 512:(s + 1) * 512]),
                    waits=wx, name=f"x1rl{h}_{s}_{tt}", dma=True, ch=f"x1rl{tt % 2}")
                wy2 = [f"dfin{h}_{s}"]
                if h == 0 and s == 0 and tt in (0, 1):
                    wy2 += ["ch1mm3_1_3"]
                if (s * 4 + tt) >= 2 or h > 0:
                    if (s * 4 + tt) >= 2:
                        py = (h, s, tt - 2) if tt >= 2 else (h, s - 1, tt + 2)
                    else:
                        py = (h - 1, 3, tt + 2)
                    wy2 += [f"y2w{py[0]}_{py[1]}_{py[2]}"]
                P.op("dve", lambda e, s=s, tt=tt, dbank=dbank: e.tensor_mul(
                    y2st[tt % 2][:, :], dbank[:, tt * 512:(tt + 1) * 512],
                    dsc_bcast[:, s * 512:(s + 1) * 512]),
                    waits=wy2 + ["ev_dscb3"], name=f"y2m{h}_{s}_{tt}")
                P.op("dve", lambda e, s=s, tt=tt, tg=tg: e.scalar_tensor_tensor(
                    y2st[tt % 2][:, :], x1rl[tg % 2][:, :], 1.0, y2st[tt % 2][:, :], ALU.mult, ALU.add),
                    waits=[f"y2m{h}_{s}_{tt}", f"x1rl{h}_{s}_{tt}"], name=f"y2a{h}_{s}_{tt}")
                P.op("dve", lambda e, s=s, tt=tt, tg=tg: e.reduce_sum(
                    sum_parts[:, tg, s:s + 1], y2st[tt % 2][:, :], axis=mybir.AxisListType.X),
                    name=f"y2rs{h}_{s}_{tt}")
                if tt > 0:
                    pq = (h, s, tt - 1)
                elif s > 0:
                    pq = (h, s - 1, 3)
                elif h > 0:
                    pq = (h - 1, 3, 3)
                else:
                    pq = None
                P.op("act", lambda e, tt=tt: e.activation(
                    sq_bf[:, 0:512], y2st[tt % 2][:, :], AF.Square),
                    waits=[f"y2a{h}_{s}_{tt}"] +
                          ([f"y2q{pq[0]}_{pq[1]}_{pq[2]}"] if pq else ["p2sq7"]),
                    name=f"y2q{h}_{s}_{tt}_sq")
                P.op("dve", lambda e, s=s, tt=tt, tg=tg: e.reduce_sum(
                    ssq_parts[:, tg, s:s + 1], sq_bf[:, 0:512], axis=mybir.AxisListType.X),
                    waits=[f"y2q{h}_{s}_{tt}_sq"], name=f"y2q{h}_{s}_{tt}")
                P.op("pool", lambda e, s=s, tg=tg, tt=tt: e.dma_start(
                    out=y2_sp[tg * 128:(tg + 1) * 128, s * 512:(s + 1) * 512], in_=y2st[tt % 2][:, :]),
                    waits=[f"y2a{h}_{s}_{tt}", f"y2q{h}_{s}_{tt}"], name=f"y2w{h}_{s}_{tt}", dma=True,
                    ch=f"y2w{tt % 2}")
            if h == 1:
                if s == 0:
                    emit_ln2_stats(0)
                    emit_y2r(0)
                    emit_y2r(1)
                elif s == 1:
                    emit_ln2_tile(0)
                    emit_y2r(2)
                elif s == 2:
                    emit_ln2_tile(1)
                    emit_y2r(3)
                else:
                    emit_ln2_tile(2)
                    emit_ln2_tile(3)

    emit_ln2_stats(1)
    emit_y2r(4)
    emit_y2r(5)
    emit_ln2_tile(4)
    emit_y2r(6)
    emit_ln2_tile(5)
    emit_y2r(7)
    emit_ln2_tile(6)
    emit_ln2_tile(7)
    P.op("sp", lambda e: None, waits=[f"outw{tg}" for tg in range(NT)] + ["y2w1_3_3"])

    from contextlib import ExitStack
    with ExitStack() as stk:
        e = stk.enter_context
        sems = {}
        for eng in Prog.ENGINES:
            if P.ccount[eng]:
                sems[("c", eng)] = e(nc.semaphore(f"sc_{eng}"))
        for i, k in enumerate(sorted(P.dch)):
            sems[k] = e(nc.semaphore(f"sd{i}_{k[1]}_{k[2]}"))
        coll_sems = [e(nc.semaphore(f"cc{i}")) for i in range(getattr(P, "ncoll", 0))]
        block = e(nc.Block())

        @block.sync
        def _(sync):
            P.emit("sp", sync, sems, coll_sems)

        @block.gpsimd
        def _(g):
            P.emit("pool", g, sems, coll_sems)

        @block.vector
        def _(v):
            P.emit("dve", v, sems, coll_sems)

        @block.scalar
        def _(s):
            P.emit("act", s, sems, coll_sems)

        @block.tensor
        def _(tens):
            P.emit("pe", tens, sems, coll_sems)

    _CACHE["nc"] = nc
    return nc


def ci_prev(gi, c):
    t = 2 * gi + c - 1
    return f"{t // 2}_{t % 2}"


def kernel(**inputs):
    x = np.ascontiguousarray(np.asarray(inputs["L0_attn_hidden_states"], np.float32))
    gi = lambda k: np.ascontiguousarray(np.asarray(inputs[k], np.float32))
    conv_w = gi("L0_attn_conv_w"); conv_b = gi("L0_attn_conv_b")
    in_proj = gi("L0_attn_in_proj_w"); ch0 = gi("L0_attn_ch_gate_w0")
    ch1 = gi("L0_attn_ch_gate_w1"); ch_b1 = gi("L0_attn_ch_gate_b1")
    mem_w = gi("L0_attn_mem_gate_w"); mem_b = gi("L0_attn_mem_gate_b")
    res_w = gi("L0_attn_residual_weight")
    gate_w = gi("L0_ffn_gate_w"); up_w = gi("L0_ffn_up_w"); down_w = gi("L0_ffn_down_w")
    gsc = gi("L0_ffn_gate_scale"); usc = gi("L0_ffn_up_scale"); dsc = gi("L0_ffn_down_scale")

    def pack_pc(v, k):
        return np.ascontiguousarray(v.reshape(k, 128).T)

    def pack_fm3(arr, inner):
        return np.ascontiguousarray(arr.reshape(16, 128, inner).transpose(1, 0, 2))

    common = dict(
        w_inprojT=np.ascontiguousarray(in_proj.T),
        w_memT=np.ascontiguousarray(mem_w.T),
        w_ch0T=np.ascontiguousarray(ch0.T),
        w_ch1T=np.ascontiguousarray(ch1.T),
        conv_w_fm=np.ascontiguousarray(conv_w.reshape(16, 128, 4).transpose(1, 0, 2).reshape(128, 64)),
        conv_b_fm=pack_pc(conv_b, 16),
        res_w_fm=pack_pc(res_w, 16),
        mem_b_col=np.ascontiguousarray(mem_b[:, None]),
        ch_b1_row=np.ascontiguousarray(ch_b1[None, :]),
        dsc_row=np.ascontiguousarray(dsc[None, :]),
        gsc_fm=pack_pc(gsc, 48),
        usc_fm=pack_pc(usc, 48),
    )
    nc = build_nc()
    in_maps = []
    for c in range(8):
        seq, p = c // 4, c % 4
        xm = np.ascontiguousarray(x[seq, p * 1024:(p + 1) * 1024])
        if p > 0:
            halo = x[seq, p * 1024 - 3:p * 1024]
        else:
            halo = np.zeros((3, 2048), np.float32)
        halo_fm = np.ascontiguousarray(halo.T.reshape(16, 128, 3).transpose(1, 0, 2))
        em = np.zeros((M, 8), np.float32)
        for s in range(8):
            if s // 4 == seq and s % 4 < p:
                em[:, s] = 1.0
        m = dict(common)
        m.update(
            x_main=xm, halo_fm=halo_fm, exmask=em,
            gate_sl=np.ascontiguousarray(gate_w[:, c * 768:(c + 1) * 768]),
            up_sl=np.ascontiguousarray(up_w[:, c * 768:(c + 1) * 768]),
            down_sl=np.ascontiguousarray(down_w[c * 768:(c + 1) * 768, :]),
        )
        in_maps.append(m)
    res = run_bass_kernel_spmd(nc, in_maps, core_ids=list(range(8)))
    outs = [np.asarray(res.results[c]["out"], np.float32) for c in range(8)]
    full = np.stack([np.concatenate(outs[0:4], 0), np.concatenate(outs[4:8], 0)], 0)
    return full


# version 33
# speedup vs baseline: 1.0207x; 1.0207x over previous
import numpy as np
import ml_dtypes
import concourse.bass as bass
import concourse.mybir as mybir
from concourse.bass_utils import run_bass_kernel_spmd

F32 = mybir.dt.float32
BF16 = mybir.dt.bfloat16
FP8 = mybir.dt.float8e4
AF = mybir.ActivationFunctionType
ALU = mybir.AluOpType

D = 2048
T = 1024
M = 64
NG = 4
GL = 256
CH = 128
NT = 8
SB = 16
NIC = 48
TH = 512
SCALE = 0.125
CLAMP = 40.0
KB = 1024


class Prog:
    ENGINES = ("pe", "dve", "act", "pool", "sp")

    def __init__(self):
        self.ops = []
        self.events = {}
        self.ccount = {e: 0 for e in self.ENGINES}
        self.dch = {}
        self.dlast = {}

    def __init_coll(self):
        pass

    def op(self, eng, fn, waits=(), name=None, dma=False, coll=False, ch=None):
        for w in waits:
            if w not in self.events:
                raise KeyError(f"wait on undefined event {w!r} (op {name!r})")
        if name is not None:
            if name in self.events:
                raise KeyError(f"duplicate event {name!r}")
            if coll:
                idx = getattr(self, "ncoll", 0)
                self.ncoll = idx + 1
                self.events[name] = (("coll", idx), 1, eng)
            elif dma:
                k = ("d", eng, ch if ch is not None else "_")
                n = self.dch.get(k, 0) + 1
                self.dch[k] = n
                prev = self.dlast.get(k)
                if prev is not None and prev not in waits:
                    waits = tuple(waits) + (prev,)
                self.dlast[k] = name
                self.events[name] = (k, 16 * n, eng)
            else:
                self.ccount[eng] += 1
                self.events[name] = (("c", eng), self.ccount[eng], eng)
        self.ops.append(dict(eng=eng, fn=fn, waits=tuple(waits), name=name, dma=dma, coll=coll))

    def emit(self, eng_name, eng, sems, coll_sems):
        hw = {}
        for rec in self.ops:
            if rec["eng"] != eng_name:
                continue
            for w in rec["waits"]:
                key, val, peng = self.events[w]
                if key[0] == "coll":
                    if hw.get(key, -1) >= 1:
                        continue
                    eng.wait_ge(coll_sems[key[1]], 1)
                    hw[key] = 1
                    continue
                if key[0] == "c" and peng == eng_name:
                    continue
                if hw.get(key, -1) >= val:
                    continue
                eng.wait_ge(sems[key], val)
                hw[key] = val
            instr = rec["fn"](eng)
            if rec["name"] is not None:
                key, _, _ = self.events[rec["name"]]
                if key[0] == "coll":
                    instr.then_inc(coll_sems[key[1]])
                else:
                    instr.then_inc(sems[key], 16 if key[0] == "d" else 1)


_CACHE = {}


def build_nc():
    if "nc" in _CACHE:
        return _CACHE["nc"]
    nc = bass.Bass("TRN2", target_bir_lowering=False)

    def din(name, shape):
        return nc.dram_tensor(name, list(shape), F32, kind="ExternalInput").ap()

    x_main = din("x_main", (T, D))
    halo_fm = din("halo_fm", (128, 16, 3))
    w_inprojT = din("w_inprojT", (D, 128))
    w_memT = din("w_memT", (D, 128))
    w_ch0T = din("w_ch0T", (D, 128))
    w_ch1T = din("w_ch1T", (128, D))
    conv_w_fm = din("conv_w_fm", (128, 64))
    conv_b_fm = din("conv_b_fm", (128, 16))
    res_w_fm = din("res_w_fm", (128, 16))
    mem_b_col = din("mem_b_col", (128, 1))
    ch_b1_row = din("ch_b1_row", (1, D))
    dsc_row = din("dsc_row", (1, D))
    gsc_fm = din("gsc_fm", (128, 48))
    usc_fm = din("usc_fm", (128, 48))
    exmask = din("exmask", (M, 8))
    gate_sl = din("gate_sl", (D, 768))
    up_sl = din("up_sl", (D, 768))
    down_sl = din("down_sl", (768, D))

    out = nc.dram_tensor("out", [T, D], F32, kind="ExternalOutput").ap()

    ident_f = nc.inline_tensor(np.eye(128, dtype=np.float32), "ident_f").ap()
    trimask_np = np.tril(np.ones((128, 128), dtype=np.float32))
    trimask = nc.inline_tensor(trimask_np, "trimask").ap()

    def dint(name, shape, dt):
        return nc.dram_tensor(name, list(shape), dt, kind="Internal").ap()

    y1_sp = dint("y1_sp", (T, D), F32)
    x1_sp = dint("x1_sp", (T, D), F32)
    y2_sp = dint("y2_sp", (T, D), F32)
    cin_g = dint("cin_g", (128, 12288), BF16)
    cout_g = dint("cout_g", (1024, 12288), BF16)
    cin_u = dint("cin_u", (128, 12288), BF16)
    cout_u = dint("cout_u", (1024, 12288), BF16)
    cin_d = dint("cin_d", (768, D), BF16)
    cout_d = dint("cout_d", (NI := 6144, D), BF16)
    cin_x = dint("cin_x", (M + 2, D), BF16)
    cout_x = dint("cout_x", (8 * (M + 2), D), BF16)

    alloc = nc.alloc_sbuf_tensor_at
    cur = [18 * KB]

    def at(name, shape, dt, o=None):
        if o is None:
            o = cur[0]
        t = alloc(name, list(shape), dt, offset=o)
        nbytes = int(np.prod(shape[1:])) * {F32: 4, BF16: 2, FP8: 1}[dt]
        cur[0] = max(cur[0], o + ((nbytes + 31) // 32) * 32)
        return t

    ident_sb = at("ident_sb", (128, 128), F32)
    trimask_sb = at("trimask_sb", (128, 128), F32)
    ones_1x128_bf = at("ones_1x128_bf", (1, 128), BF16)
    ones_1x64_f = at("ones_1x64_f", (1, 64), F32)
    ones_c64 = at("ones_c64", (64, 1), F32)
    mem_b_sb = at("mem_b_sb", (128, 1), F32)
    conv_b_sb = at("conv_b_sb", (128, 16), F32)
    res_w_sb = at("res_w_sb", (128, 16), F32)
    conv_w_sb = at("conv_w_sb", (128, 64), F32)
    gsc_sb = at("gsc_sb", (128, 48), F32)
    usc_sb = at("usc_sb", (128, 48), F32)
    eps_col = at("eps_col", (128, 1), F32)
    exmask_sb = at("exmask_sb", (M, 8), F32)
    ch_b1_bf = at("ch_b1_bf", (1, D), BF16)
    dsc_bf = at("dsc_bf", (1, D), BF16)
    dsc_bcast = at("dsc_bcast", (128, D), BF16)
    ones_g = at("ones_g", (M, GL), F32)
    _bsb0 = cur[0]
    beta_sb = at("beta_sb", (M, 8), F32)
    beta_sb_bits = at("beta_sb_bits", (M, 16), BF16, _bsb0)
    _brn0 = cur[0]
    beta_run = at("beta_run", (M, 1), F32)
    beta_bits = at("beta_bits", (M, 2), BF16, _brn0)
    dfac_sb = at("dfac_sb", (M, 8), F32)
    rB_store = at("rB_store", (M, T), BF16)
    sum_parts = at("sum_parts", (128, NT, 4), F32)
    ssq_parts = at("ssq_parts", (128, NT, 4), F32)
    ln8_a = at("ln8_a", (128, 8), F32)
    ln8_b = at("ln8_b", (128, 8), F32)
    ln8_c = at("ln8_c", (128, 8), F32)

    S_f = at("S_f", (M, D), F32)
    S_bf = at("S_bf", (M, D), BF16)
    B_g = at("B_g", (M, GL), F32)
    r_s = at("r_s", (M, GL), F32)
    k_raw = at("k_raw", (M, GL), F32)
    kg_f = at("kg_f", (M, GL), F32)
    sel_b = at("sel_b", (M, GL), F32)
    tau_b = at("tau_b", (M, GL), F32)
    it_b = at("it_b", (M, GL), F32)
    gm_tmp = at("gm_tmp", (M, GL), F32)
    kn_row = at("kn_row", (1, GL), F32)
    t1_bf = at("t1_bf", (128, GL), BF16)
    eb_c = at("eb_c", (M, CH), F32)
    eB_c = at("eB_c", (M, CH), F32)
    arg_c = at("arg_c", (M, CH), F32)
    ek_c = at("ek_c", (M, SB), F32)
    negb = at("negb", (M, 1), F32)
    rb_bf = at("rb_bf", (M, CH), BF16)
    rhat_bf = at("rhat_bf", (M, CH), BF16)
    khat_bf = at("khat_bf", (M, SB), BF16)
    K2_f = at("K2_f", (M, CH), F32)
    K2T_bf = at("K2T_bf", (CH, M), BF16)
    PT_sb = at("PT_sb", (128, 128), F32)
    P_sb = at("P_sb", (128, 128), BF16)
    ident_bf = at("ident_bf", (128, 128), BF16)
    pkarg = [at(f"pkarg{i}", (M, 8, SB), F32) for i in range(2)]
    pargf = [at(f"pargf{i}", (M, 8, CH), F32) for i in range(2)]
    pkh = [at(f"pkh{i}", (M, 8, SB), BF16) for i in range(2)]
    prh = [at(f"prh{i}", (M, 8, CH), BF16) for i in range(2)]
    fence_sc = at("fence_sc", (128, 4), F32)
    fence_big = at("fence_big", (128, 512), F32)

    A0 = cur[0]
    y1b = [at(f"y1b{i}", (128, D), F32, A0 + i * 8 * KB) for i in range(2)]
    fuse = at("fuse", (128, NIC, TH), BF16, A0)
    diag_bf = at("diag_bf", (128, 16, 128), BF16, A0 + 16 * KB)
    cur[0] = A0 + 48 * KB

    C0 = cur[0]
    x_fm = at("x_fm", (128, 16, 272), BF16, C0)
    shift_fm = at("shift_fm", (128, 16, GL), BF16, C0 + 8704)
    u_bf = [at(f"u_bf{i}", (128, D), BF16, C0 + 16896 + i * 4 * KB) for i in range(2)]
    sres_tm = [at(f"sres{i}", (128, D), BF16, C0 + 25088 + i * 4 * KB) for i in range(2)]
    x_bf = [at(f"x_bf{i}", (128, D), BF16, C0 + 33280 + i * 4 * KB) for i in range(2)]
    convtmp = [at(f"convtmp{i}", (128, GL), F32, C0 + 41472 + i * KB) for i in range(2)]
    Zs_bf = [at(f"Zs{i}", (M, D), BF16, C0 + i * 4 * KB) for i in range(2)]
    T_state = at("T_state", (M, D), F32, C0 + 8 * KB)
    S_init_bf = at("S_init_bf", (M, D), BF16, C0 + 16 * KB)
    gst = [at(f"gst{i}", (128, 16, 256), BF16, C0 + 21 * KB + i * 8 * KB) for i in range(2)]
    ust = [at(f"ust{i}", (128, 16, 256), BF16, C0 + i * 8 * KB) for i in range(2)]
    sres_stage = at("sres_stage", (128, 16, 128), BF16, C0 + 43 * KB + 512)
    x1bf_t = at("x1bf_t", (128, D), BF16, C0 + 37 * KB)
    y2rl = [at(f"y2rl{i}", (128, D), F32, C0 + 21 * KB + i * 8 * KB) for i in range(2)]
    uscr = at("uscr", (128, TH), BF16, C0 + 41 * KB)
    sq_bf = at("sq_bf", (128, D), BF16, C0 + 43 * KB + 512)
    cur[0] = C0 + 48 * KB

    B0 = cur[0]
    q8_g = at("q8_g", (128, 8, 768), FP8, B0)
    qbf_g = at("qbf_g", (128, 8, 768), BF16, B0 + 6 * KB)
    q8_d = at("q8_d", (128, 3, D), FP8, B0)
    qbf_d = at("qbf_d", (128, 3, D), BF16, B0 + 6 * KB)
    x1_fm = at("x1_fm", (128, 16, T), BF16, B0)
    cur[0] = B0 + 32 * KB

    D0 = cur[0]
    w_ip_sb = at("w_ip_sb", (128, 16, 128), BF16, D0)
    w_mg_sb = at("w_mg_sb", (128, 16, 128), BF16, D0 + 4 * KB)
    w_c0_sb = at("w_c0_sb", (128, 16, 128), BF16, D0 + 8 * KB)
    w_c1_sb = at("w_c1_sb", (128, D), BF16, D0 + 12 * KB)
    dst_b = [at(f"dst{i}", (128, 4, 512), BF16, D0 + i * 4 * KB) for i in range(2)]
    x1rl = [at(f"x1rl{i}", (128, 512), F32, D0 + 8 * KB + i * 2 * KB) for i in range(2)]
    y2st = [at(f"y2st{i}", (128, 512), F32, D0 + 12 * KB + i * 2 * KB) for i in range(2)]
    cur[0] = D0 + 16 * KB
    assert cur[0] < 223 * KB, f"SBUF over budget: {cur[0]}"

    O_ps = nc.alloc_psum_tensor("O_ps", [128, 2048], F32)
    A_ps = nc.alloc_psum_tensor("A_ps", [128, 2048], F32)

    P = Prog()

    def fdve(nm, ap=None, p=64):
        P.op("dve", lambda e: e.tensor_copy(fence_big[:, 0:256], fence_big[:, 256:512]), name=nm)

    def fact(nm, ap=None, p=64):
        P.op("act", lambda e: e.activation(fence_big[:, 256:512], fence_big[:, 0:256], AF.Copy), name=nm)

    def gapD():
        P.op("dve", lambda e: e.tensor_copy(fence_big[:, 0:256], fence_big[:, 256:512]))

    def gapA():
        P.op("act", lambda e: e.activation(fence_big[:, 256:512], fence_big[:, 0:256], AF.Copy))

    P.op("sp", lambda e: e.dma_start(out=ident_sb[:], in_=ident_f), name="ld_ident", dma=True, ch="ld_ident")
    P.op("sp", lambda e: e.dma_start(out=trimask_sb[:], in_=trimask), name="ld_tri", dma=True, ch="ld_tri")
    for nm, dst, src in [("mb", mem_b_sb, mem_b_col), ("cb", conv_b_sb, conv_b_fm),
                         ("rw", res_w_sb, res_w_fm), ("cw", conv_w_sb, conv_w_fm),
                         ("gs", gsc_sb, gsc_fm), ("us", usc_sb, usc_fm),
                         ("xm", exmask_sb, exmask)]:
        P.op("sp", lambda e, d=dst, s=src: e.dma_start(out=d[:], in_=s), name=f"ld_{nm}", dma=True, ch=f"ld_{nm}")
    P.op("pool", lambda e: e.dma_start(out=x_fm[:, :, 13:16], in_=halo_fm), name="ld_halo", dma=True, ch="ld_halo")
    for nm, dst, src in [("ip", w_ip_sb, w_inprojT.rearrange("(a p) m -> p a m", p=128)),
                         ("mg", w_mg_sb, w_memT.rearrange("(a p) m -> p a m", p=128)),
                         ("c0", w_c0_sb, w_ch0T.rearrange("(a p) m -> p a m", p=128)),
                         ("c1", w_c1_sb, w_ch1T),
                         ("b1", ch_b1_bf, ch_b1_row),
                         ("dsc", dsc_bf, dsc_row)]:
        P.op("pool", lambda e, d=dst, s=src: e.dma_start(out=d[:], in_=s), name=f"cld_{nm}", dma=True, ch=f"cld_{nm}")
    for nm, buf, v in [("ones", ones_g, 1.0), ("o64", ones_1x64_f, 1.0), ("oc64", ones_c64, 1.0),
                       ("o128", ones_1x128_bf, 1.0), ("eps", eps_col, 1e-5), ("S", S_f, 0.0),
                       ("br", beta_run, 0.0)]:
        P.op("dve", lambda e, b=buf, v=v: e.memset(b[:], v), name=f"ms_{nm}")
    for s in range(4):
        P.op("pe", lambda e, s=s: e.matmul(A_ps[:, s * 512:(s + 1) * 512], ones_1x128_bf[:],
                                           dsc_bf[:, s * 512:(s + 1) * 512], start=True, stop=True),
             waits=["ms_o128", "cld_dsc"], name=f"mm_dscb{s}")
        P.op("dve", lambda e, s=s: e.tensor_copy(dsc_bcast[:, s * 512:(s + 1) * 512],
                                                 A_ps[:, s * 512:(s + 1) * 512]),
             waits=[f"mm_dscb{s}"], name=f"ev_dscb{s}")
    P.op("dve", lambda e: e.tensor_copy(ident_bf[:], ident_sb[:]), waits=["ld_ident"], name="ident_bf")
    for a in range(16):
        P.op("dve", lambda e, a=a: e.tensor_scalar(diag_bf[:, a, :], ident_sb[:, :],
                                                   res_w_sb[:, a:a + 1], None, ALU.mult),
             waits=(["ld_ident", "ld_rw"] if a == 0 else []), name=f"diag{a}")

    def xload(t):
        wy = [f"y1_spill{t - 2}"] if t >= 2 else []
        P.op("sp", lambda e, t=t: e.dma_start(out=y1b[t % 2][:], in_=x_main[t * 128:(t + 1) * 128, :]),
             waits=wy, name=f"xld{t}", dma=True, ch=f"xld{t % 2}")
        wb = [f"xbf_done{t - 2}"] if t >= 2 else []
        P.op("pool", lambda e, t=t: e.dma_start(out=x_bf[t % 2][:], in_=x_main[t * 128:(t + 1) * 128, :]),
             waits=wb, name=f"xbf{t}", dma=True, ch=f"xbf{t % 2}")

    xload(0)
    xload(1)

    def quant(tag, src_ap, cin_ap, q8, qbf):
        for ss in range(2):
            if tag == "d":
                src = src_ap[ss * 384:(ss + 1) * 384, :].rearrange("(c p) d -> p c d", p=128)
                cdst = cin_ap[ss * 384:(ss + 1) * 384, :].rearrange("(c p) d -> p c d", p=128)
            else:
                src = src_ap.rearrange("(a p) i -> p a i", p=128)[:, ss * 8:(ss + 1) * 8, :]
                cdst = cin_ap[:, ss * 6144:(ss + 1) * 6144].rearrange("p (a i) -> p a i", i=768)
            prev = {"g": ("ev_dscb3",), "u": ("qc_g1",), "d": ("ev_dscb3",)}[tag] if ss == 0 else (f"qc_{tag}0",)
            P.op("pool", lambda e, s=src, d=q8: e.dma_start(out=d[:, :, :], in_=s),
                 waits=list(prev), name=f"q8_{tag}{ss}", dma=True, ch="q8")
            P.op("dve", lambda e, d=qbf, s=q8: e.tensor_copy(d[:, :, :], s[:, :, :]),
                 waits=[f"q8_{tag}{ss}"], name=f"qb_{tag}{ss}")
            P.op("act", lambda e, c=cdst, d=qbf: e.dma_start(out=c, in_=d[:, :, :]),
                 waits=[f"qb_{tag}{ss}"], name=f"qc_{tag}{ss}", dma=True, ch="qc")

    def emit_ag(tag, ci, co):
        P.op("pool", lambda e, ci=ci, co=co: e.collective_compute(
            "AllGather", ALU.bypass, replica_groups=[list(range(8))],
            ins=[ci.opt()], outs=[co.opt()]),
            waits=[f"qc_{tag}1"], name=f"ag_{tag}", coll=True)

    quant("g", gate_sl, cin_g, q8_g, qbf_g)
    emit_ag("g", cin_g, cout_g)

    mixw = ["cld_ip", "cld_mg", "cld_c0", "cld_c1", "cld_b1", "ld_cw", "ld_cb", "ld_rw", "ld_mb"]

    for gi in range(NG):
        if gi == 1:
            quant("u", up_sl, cin_u, q8_g, qbf_g)
            emit_ag("u", cin_u, cout_u)
        if gi == 2:
            quant("d", down_sl, cin_d, q8_d, qbf_d)
            emit_ag("d", cin_d, cout_d)
        t0, t1 = 2 * gi, 2 * gi + 1
        prevgrp = [f"ch0mm{gi - 1}", f"conv_done{gi - 1}"] if gi > 0 else []
        if gi > 0:
            P.op("dve", lambda e: e.tensor_copy(x_fm[:, :, 13:16], x_fm[:, :, 269:272]),
                 waits=prevgrp, name=f"halo{gi}")
        halo_ev = f"halo{gi}" if gi > 0 else "ld_halo"
        for sl, t in enumerate((t0, t1)):
            for j in range(16):
                wtr = []
                if j == 0:
                    wtr = [f"xbf{t}", "ident_bf"]
                    wtr += [f"xfme{t - 1}"] if sl == 1 else (
                        [f"y1acc{t0 - 1}"] if gi > 0 else [])
                P.op("pe", lambda e, j=j, t=t: e.matmul(
                    O_ps[:, j * 128:(j + 1) * 128], x_bf[t % 2][:, j * 128:(j + 1) * 128],
                    ident_bf[:, :], start=True, stop=True),
                    waits=wtr, name=(f"xtr{t}" if j == 15 else None))
            for q4 in range(4):
                we = [f"xtr{t}"]
                if q4 == 0:
                    we += prevgrp if sl == 0 else [halo_ev]
                P.op("act", lambda e, q4=q4, sl=sl: e.activation(
                    x_fm[:, 4 * q4:4 * q4 + 4, 16 + sl * 128:16 + (sl + 1) * 128],
                    O_ps[:, q4 * 512:(q4 + 1) * 512].rearrange("p (a n) -> p a n", n=128),
                    AF.Copy), waits=we, name=(f"xfme{t}" if q4 == 3 else None))
        for a in range(16):
            ct = convtmp[a % 2]
            wv = [f"xfme{t0}", f"xfme{t1}"] + (mixw if gi == 0 else [])
            if a >= 2 or gi > 0:
                wv += [f"silu{gi}_{a - 2}"] if a >= 2 else [f"silu{gi - 1}_{14 + a}"]
            P.op("dve", lambda e, a=a, ct=ct: e.tensor_scalar(
                ct[:, :], x_fm[:, a, 13:13 + GL], conv_w_sb[:, 4 * a:4 * a + 1], None, ALU.mult),
                waits=wv, name=f"cv{gi}_{a}_0")
            for i in range(1, 4):
                P.op("dve", lambda e, a=a, i=i, ct=ct: e.scalar_tensor_tensor(
                    ct[:, :], x_fm[:, a, 13 + i:13 + i + GL], conv_w_sb[:, 4 * a + i:4 * a + i + 1],
                    ct[:, :], ALU.mult, ALU.add), name=f"cv{gi}_{a}_{i}")
            wsilu = [f"cv{gi}_{a}_3"]
            if gi > 0:
                wsilu += [f"sresmm{t0 - 1}"]
            P.op("act", lambda e, a=a, ct=ct: e.activation(
                shift_fm[:, a, :], ct[:, :], AF.Silu, bias=conv_b_sb[:, a:a + 1]),
                waits=wsilu, name=f"silu{gi}_{a}")
        P.op("dve", lambda e: e.engine_nop(), waits=[f"cv{gi}_15_3"], name=f"conv_done{gi}")

        rk_ps = A_ps[:, 1536:1792]
        mg_ps = A_ps[:, 1024:1280]
        ch0_ps = A_ps[:, 1792:2048]
        wrk = [f"silu{gi}_{a}" for a in range(16)]
        if gi > 0:
            wrk += [f"rk_ev{gi - 1}", f"k_ev{gi - 1}", f"t1_ev{gi - 1}", f"sig{gi - 1}_1_2"]
        for a in range(16):
            P.op("pe", lambda e, a=a: e.matmul(rk_ps, w_ip_sb[:, a, :], shift_fm[:, a, :],
                                               start=(a == 0), stop=(a == 15)),
                 waits=(wrk if a == 0 else []), name=(f"rkmm{gi}" if a == 15 else None))
        wmg = [f"mg1_ev{gi - 1}", f"tau_ev{gi - 1}", f"sig{gi - 1}_1_3"] if gi > 0 else []
        for a in range(16):
            P.op("pe", lambda e, a=a: e.matmul(mg_ps, w_mg_sb[:, a, :], shift_fm[:, a, :],
                                               start=(a == 0), stop=(a == 15)),
                 waits=(wmg if a == 0 else []), name=(f"mgmm{gi}" if a == 15 else None))
        P.op("act", lambda e: e.activation(r_s[:, :], rk_ps[0:64, :], AF.Copy, scale=SCALE),
             waits=[f"rkmm{gi}"], name=f"rk_ev{gi}")
        P.op("act", lambda e: e.activation(k_raw[:, :], rk_ps[64:128, :], AF.Copy), name=f"k_ev{gi}")
        P.op("act", lambda e: e.activation(gm_tmp[:, :], mg_ps[0:64, :], AF.Identity, bias=mem_b_sb[0:64, :]),
             waits=[f"mgmm{gi}"], name=f"mg1_ev{gi}")
        P.op("act", lambda e: e.activation(tau_b[:, :], mg_ps[64:128, :], AF.Sigmoid,
                                           bias=mem_b_sb[64:128, :]), name=f"tau_ev{gi}")
        for a in range(16):
            w0 = [f"rk_ev{gi}", f"k_ev{gi}", f"xfme{t1}"] if a == 0 else []
            if a == 0 and gi > 0:
                w0 += [f"t1_ev{gi - 1}"]
            P.op("pe", lambda e, a=a: e.matmul(ch0_ps, w_c0_sb[:, a, :], x_fm[:, a, 16:16 + GL],
                                               start=(a == 0), stop=(a == 15)),
                 waits=w0, name=(f"ch0mm{gi}" if a == 15 else None))
        wt1 = [f"ch0mm{gi}"]
        if gi > 0:
            wt1 += [f"ch1mm{gi - 1}_1_3"]
        P.op("act", lambda e: e.activation(t1_bf[:, :], ch0_ps[:, :], AF.Copy), waits=wt1, name=f"t1_ev{gi}")

        P.op("dve", lambda e: e.tensor_scalar(sel_b[:, :], gm_tmp[:, :], 0.0, None, ALU.max),
             waits=[f"mg1_ev{gi}"], name=f"gmax{gi}")
        P.op("dve", lambda e: e.tensor_scalar(it_b[:, :], gm_tmp[:, :], -1.0, None, ALU.mult),
             name=f"gneg{gi}")
        gapD()
        P.op("dve", lambda e: e.tensor_max(it_b[:, :], it_b[:, :], gm_tmp[:, :]),
             name=f"gabs{gi}")
        P.op("act", lambda e: e.activation(it_b[:, :], it_b[:, :], AF.Exp, scale=-1.0),
             waits=[f"gabs{gi}"], name=f"gexp{gi}")
        P.op("dve", lambda e: e.tensor_scalar(it_b[:, :], it_b[:, :], 1.0, None, ALU.add),
             waits=[f"gexp{gi}"], name=f"gp1{gi}")
        P.op("act", lambda e: e.activation(it_b[:, :], it_b[:, :], AF.Ln),
             waits=[f"gp1{gi}"], name=f"gln{gi}")
        P.op("dve", lambda e: e.tensor_add(sel_b[:, :], sel_b[:, :], it_b[:, :]),
             waits=[f"gln{gi}", f"gmax{gi}"], name=f"sel{gi}")
        P.op("dve", lambda e: e.scalar_tensor_tensor(gm_tmp[:, :], sel_b[:, :], -1.0, tau_b[:, :],
                                                     ALU.mult, ALU.mult),
             waits=[f"sel{gi}", f"tau_ev{gi}"], name=f"g{gi}")
        gapD()
        wB = [f"g{gi}", "ms_ones"]
        if gi > 0:
            wB += [f"eB{gi - 1}_1", f"eb{gi - 1}_1", f"ek2{gi - 1}_done"]
        P.op("dve", lambda e: e.tensor_tensor_scan(B_g[:, :], ones_g[:, :], gm_tmp[:, :], 0.0,
                                                   ALU.mult, ALU.add), waits=wB, name=f"B{gi}")
        P.op("act", lambda e: e.activation(it_b[:, :], sel_b[:, :], AF.Ln), waits=[f"sel{gi}"],
             name=f"lnsel{gi}")
        P.op("dve", lambda e: e.tensor_mul(it_b[:, :], it_b[:, :], tau_b[:, :]),
             waits=[f"lnsel{gi}"], name=f"itm{gi}")
        P.op("act", lambda e: e.activation(it_b[:, :], it_b[:, :], AF.Exp), waits=[f"itm{gi}"],
             name=f"it{gi}")
        P.op("act", lambda e: e.activation(kg_f[:, :], k_raw[:, :], AF.Square), waits=[f"k_ev{gi}"],
             name=f"ksq{gi}")
        knorm_ps = A_ps[0:1, 1280:1536]
        kb_ps = A_ps[0:64, 1280:1536]
        wkn = [f"ksq{gi}", "ms_oc64"]
        P.op("pe", lambda e, knorm_ps=knorm_ps: e.matmul(knorm_ps, ones_c64[:, :], kg_f[:, :],
                                                         start=True, stop=True),
             waits=wkn, name=f"knmm{gi}")
        P.op("act", lambda e: e.activation(kn_row[0:1, :], knorm_ps, AF.Ln), waits=[f"knmm{gi}"],
             name=f"knln{gi}")
        gapA()
        P.op("act", lambda e: e.activation(kn_row[0:1, :], kn_row[0:1, :], AF.Exp, scale=-0.5),
             name=f"knexp{gi}_raw")
        fact(f"knexp{gi}", kn_row[0:1, 0:1], 1)
        P.op("dve", lambda e: e.tensor_scalar(kn_row[0:1, :], kn_row[0:1, :], 1e12, None, ALU.min),
             waits=[f"knexp{gi}"], name=f"knmin{gi}_raw")
        fdve(f"knmin{gi}", kn_row[0:1, 0:1], 1)
        P.op("pe", lambda e: e.matmul(kb_ps, ones_1x64_f[:], kn_row[0:1, :], start=True, stop=True),
             waits=[f"knmin{gi}", "ms_o64", f"knln{gi}"], name=f"knbc{gi}")
        P.op("dve", lambda e: e.tensor_mul(kg_f[:, :], k_raw[:, :], kb_ps), waits=[f"knbc{gi}"],
             name=f"kn{gi}")
        P.op("dve", lambda e: e.tensor_mul(kg_f[:, :], kg_f[:, :], it_b[:, :]),
             waits=[f"it{gi}", f"kn{gi}"], name=f"kg{gi}")

        for c in range(2):
            t = 2 * gi + c
            cs = c * CH
            ci = f"{gi}_{c}"
            ch1_main = A_ps[:, 1536:2048]
            ch1_alt = A_ps[:, 1024:1536]
            for s4 in range(4):
                ps = ch1_main if s4 % 2 == 0 else ch1_alt
                wch = []
                if s4 >= 2:
                    wch = [f"sig{ci}_{s4 - 2}"]
                if s4 == 1 and c == 0:
                    wch = [f"kn{gi}"]
                if s4 == 0:
                    wch = [f"t1_ev{gi}", "cld_c1", "cld_b1", "ms_o128"]
                    if t >= 2:
                        wch += [f"ulast{t - 2}"]
                    if t >= 1:
                        pt = t - 1
                        wch += [f"sig{pt // 2}_{pt % 2}_3"]
                    if c == 0:
                        wch += [f"mg1_ev{gi}", f"tau_ev{gi}"]
                P.op("pe", lambda e, s4=s4, ps=ps, c=c: e.matmul(
                    ps, t1_bf[:, c * 128:(c + 1) * 128], w_c1_sb[:, s4 * 512:(s4 + 1) * 512],
                    start=True, stop=False), waits=wch)
                P.op("pe", lambda e, s4=s4, ps=ps: e.matmul(
                    ps, ones_1x128_bf[:], ch_b1_bf[:, s4 * 512:(s4 + 1) * 512],
                    start=False, stop=True), name=f"ch1mm{ci}_{s4}")
                P.op("act", lambda e, s4=s4, ps=ps, t=t: e.activation(
                    u_bf[t % 2][:, s4 * 512:(s4 + 1) * 512], ps, AF.Sigmoid),
                    waits=[f"ch1mm{ci}_{s4}"], name=f"sig{ci}_{s4}")
            P.op("dve", lambda e, t=t: e.tensor_mul(u_bf[t % 2][:], u_bf[t % 2][:], x_bf[t % 2][:]),
                 waits=[f"sig{ci}_3", f"xbf{t}"], name=f"uprod{t}")
            P.op("dve", lambda e: e.engine_nop(), waits=[f"uprod{t}", f"xtr{t}"], name=f"xbf_done{t}")

            if c == 0:
                P.op("dve", lambda e: e.memset(negb[:], 0.0), waits=[f"B{gi}"], name=f"negb{ci}_raw")
            else:
                P.op("dve", lambda e, cs=cs: e.tensor_scalar(negb[:], B_g[:, cs - 1:cs], -1.0, None,
                                                             ALU.mult), waits=[f"B{gi}"], name=f"negb{ci}_raw")
            fdve(f"negb{ci}", negb[:, :])
            weB = [f"negb{ci}"]
            if t > 0:
                weB += [f"state{t - 1}"]
            P.op("act", lambda e, cs=cs: e.activation(eb_c[:, :], B_g[:, cs:cs + CH], AF.Exp,
                                                      bias=negb[:, :]), waits=weB, name=f"eb{ci}")
            P.op("dve", lambda e, cs=cs: e.tensor_mul(rb_bf[:, :], r_s[:, cs:cs + CH], eb_c[:, :]),
                 waits=[f"eb{ci}", f"rk_ev{gi}"], name=f"rb{ci}")
            wsb = [f"state{t - 1}"] if t > 0 else ["ms_S"]
            P.op("act", lambda e: e.activation(S_bf[:, :], S_f[:, :], AF.Copy), waits=wsb, name=f"sbf{ci}")
            for s4 in range(4):
                wo = []
                if s4 == 0:
                    wo = [f"rb{ci}", f"sbf{ci}", f"xfme{2 * gi + 1}"]
                    if t >= 1:
                        wo += [f"y1acc{t - 1}"]
                P.op("pe", lambda e, s4=s4: e.matmul(
                    O_ps[:, s4 * 512:(s4 + 1) * 512], rb_bf[:, :], S_bf[:, s4 * 512:(s4 + 1) * 512],
                    start=True, stop=False), waits=wo, name=(f"ointer{ci}" if s4 == 3 else None))
            for a in range(16):
                P.op("pe", lambda e, a=a, c=c: e.matmul(
                    O_ps[:, a * 128:(a + 1) * 128], shift_fm[:, a, c * 128:(c + 1) * 128],
                    diag_bf[:, a, :], start=False, stop=False),
                    waits=(["diag15"] if a == 0 else []), name=(f"sresmm{t}" if a == 15 else None))
            PT_ps = A_ps[:, 0:128]
            pb = t % 2
            wprep = [f"B{gi}"]
            if t >= 2:
                wprep += [f"ptmm{gi - 1}_{c}"]
            for I in range(8):
                ie = cs + I * SB + SB - 1
                P.op("dve", lambda e, I=I, cs=cs, ie=ie, pb=pb: e.tensor_scalar(
                    pkarg[pb][:, I, :], B_g[:, cs + I * SB:cs + I * SB + SB],
                    B_g[:, ie:ie + 1], None, ALU.subtract),
                    waits=(wprep if I == 0 else []), name=f"eka{ci}_{I}")
                P.op("dve", lambda e, I=I, cs=cs, ie=ie, pb=pb: e.tensor_scalar(
                    pargf[pb][:, I, :], B_g[:, cs:cs + CH], B_g[:, ie:ie + 1], CLAMP,
                    ALU.subtract, ALU.min), name=f"parg{ci}_{I}")
            fdve(f"pprep{ci}")
            P.op("act", lambda e, pb=pb: e.activation(pkarg[pb][:, :, :], pkarg[pb][:, :, :],
                                                      AF.Exp, scale=-1.0),
                 waits=[f"pprep{ci}"], name=f"ekx{ci}")
            P.op("act", lambda e, pb=pb: e.activation(pargf[pb][:, :, :], pargf[pb][:, :, :],
                                                      AF.Exp), name=f"erhx{ci}")
            for I in range(8):
                P.op("dve", lambda e, I=I, cs=cs, pb=pb: e.tensor_mul(
                    pkh[pb][:, I, :], kg_f[:, cs + I * SB:cs + I * SB + SB], pkarg[pb][:, I, :]),
                    waits=([f"ekx{ci}", f"kg{gi}"] if I == 0 else []), name=f"pkh{ci}_{I}")
                P.op("dve", lambda e, I=I, cs=cs, pb=pb: e.tensor_mul(
                    prh[pb][:, I, :], r_s[:, cs:cs + CH], pargf[pb][:, I, :]),
                    waits=([f"erhx{ci}"] if I == 0 else []), name=f"prh{ci}_{I}")
            fdve(f"prdy{ci}")
            for I in range(8):
                wpt = [f"prdy{ci}"] if I == 0 else []
                if I == 0 and t > 0:
                    wpt += [f"zdone{t - 1}"]
                P.op("pe", lambda e, I=I, pb=pb: e.matmul(
                    PT_ps[:, I * SB:(I + 1) * SB], prh[pb][:, I, :], pkh[pb][:, I, :],
                    start=True, stop=True),
                    waits=wpt, name=(f"ptmm{ci}" if I == 7 else None))
            P.op("dve", lambda e: e.tensor_mul(PT_sb[:, :], PT_ps, trimask_sb[:, :]),
                 waits=[f"ptmm{ci}", "ld_tri"], name=f"ptev{ci}")
            P2_ps = A_ps[:, 512:640]
            wp2 = [f"ptev{ci}", "ld_ident"]
            if t > 0:
                wp2 += [f"zdone{t - 1}"]
            if c == 0:
                wp2 += [f"knln{gi}", f"kn{gi}"]
            P.op("pe", lambda e: e.transpose(P2_ps, PT_sb[:, :], ident_sb[:, :]), waits=wp2,
                 name=f"p2{ci}")
            P.op("dve", lambda e: e.tensor_copy(P_sb[:, :], P2_ps), waits=[f"p2{ci}"], name=f"pev{ci}")
            for s4 in range(4):
                wintra = [f"pev{ci}", f"uprod{t}", f"ointer{ci}"] if s4 == 0 else []
                P.op("pe", lambda e, s4=s4, t=t: e.matmul(
                    O_ps[:, s4 * 512:(s4 + 1) * 512], P_sb[:, :], u_bf[t % 2][:, s4 * 512:(s4 + 1) * 512],
                    start=False, stop=True), waits=wintra, name=(f"ointra{ci}" if s4 == 3 else None))
            P.op("act", lambda e, cs=cs: e.activation(eB_c[:, :], B_g[:, cs:cs + CH], AF.Exp,
                                                      bias=beta_run[:, :]),
                 waits=[f"B{gi}", "ms_br"] + ([f"brun{gi - 1}"] if gi > 0 else []), name=f"eB{ci}")
            P.op("dve", lambda e, cs=cs, t=t: e.tensor_mul(rB_store[:, t * 128:(t + 1) * 128],
                                                           r_s[:, cs:cs + CH], eB_c[:, :]),
                 waits=[f"eB{ci}"], name=f"rB{ci}")
            P.op("dve", lambda e, t=t: e.scalar_tensor_tensor(
                y1b[t % 2][:], O_ps[:, :], 1.0, y1b[t % 2][:], ALU.mult, ALU.add),
                waits=[f"ointra{ci}", f"sresmm{t}", f"xld{t}"], name=f"y1acc{t}")
            P.op("pool", lambda e, t=t: e.dma_start(out=y1_sp[t * 128:(t + 1) * 128, :], in_=y1b[t % 2][:]),
                 waits=[f"y1acc{t}"], name=f"y1_spill{t}", dma=True, ch=f"y1sp{t % 2}")
            wek2 = [f"B{gi}"]
            if t > 0:
                wek2 += [f"state{t - 1}"]
            P.op("act", lambda e, cs=cs: e.activation(K2_f[:, :], B_g[:, cs:cs + CH], AF.Exp, scale=-1.0,
                                                      bias=B_g[:, cs + CH - 1:cs + CH]),
                 waits=wek2, name=f"ek2{ci}")
            P.op("dve", lambda e, cs=cs: e.tensor_mul(K2_f[:, :], K2_f[:, :], kg_f[:, cs:cs + CH]),
                 waits=[f"ek2{ci}", f"kg{gi}"], name=f"K2{ci}")
            if c == 1:
                P.op("dve", lambda e: e.engine_nop(), waits=[f"K2{ci}"], name=f"ek2{gi}_done")
            K2T_ps = A_ps[:, 128:192]
            wk2t = [f"K2{ci}", f"ptev{ci}", "ld_ident"]
            if t > 0:
                wk2t += [f"zdone{t - 1}"]
            P.op("pe", lambda e: e.transpose(K2T_ps, K2_f[:, :], ident_sb[0:64, 0:64]), waits=wk2t,
                 name=f"k2t{ci}")
            P.op("dve", lambda e: e.tensor_copy(K2T_bf[:, :], K2T_ps), waits=[f"k2t{ci}"],
                 name=f"k2ev{ci}")
            for p4 in range(4):
                Z_ps = A_ps[0:64, (p4 % 2) * 512:(p4 % 2) * 512 + 512]
                wz = []
                if p4 == 0:
                    wz = [f"k2ev{ci}", f"uprod{t}", f"ptev{ci}"]
                if p4 == 1:
                    wz = [f"pev{ci}"]
                    if c == 0:
                        wz += [f"kn{gi}"]
                if p4 >= 2:
                    wz = [f"st{ci}_{p4 - 2}"]
                P.op("pe", lambda e, Z=Z_ps, t=t, p4=p4: e.matmul(
                    Z, K2T_bf[:, :], u_bf[t % 2][:, p4 * 512:(p4 + 1) * 512], start=True, stop=True),
                    waits=wz, name=f"zmm{ci}_{p4}")
                wst = [f"zmm{ci}_{p4}", f"eb{ci}"]
                if t > 0 and p4 == 0:
                    wst += [f"sbf{ci}"]
                P.op("dve", lambda e, Z=Z_ps, p4=p4, cs=cs: e.scalar_tensor_tensor(
                    S_f[:, p4 * 512:(p4 + 1) * 512], S_f[:, p4 * 512:(p4 + 1) * 512],
                    eb_c[:, CH - 1:CH], Z, ALU.mult, ALU.add), waits=wst, name=f"st{ci}_{p4}")
            P.op("dve", lambda e: e.engine_nop(), waits=[f"st{ci}_3"], name=f"state{t}")
            P.op("dve", lambda e: e.engine_nop(), waits=[f"st{ci}_3"], name=f"zdone{t}")
            P.op("dve", lambda e: e.engine_nop(), waits=[f"zmm{ci}_3"], name=f"ulast{t}")
            if t + 2 < NT:
                xload(t + 2)
        P.op("dve", lambda e: e.tensor_add(beta_run[:, :], beta_run[:, :], B_g[:, GL - 1:GL]),
             waits=[f"eB{gi}_0", f"eB{gi}_1", f"B{gi}"], name=f"brun{gi}_raw")
        fdve(f"brun{gi}", beta_run[:, :])

    P.op("dve", lambda e: e.memset(T_state[:], 0.0), waits=["sresmm7"], name="ms_T")
    P.op("act", lambda e: e.activation(S_bf[:, :], S_f[:, :], AF.Copy),
         waits=["state7", "ointer3_1"], name="zseg")
    P.op("pool", lambda e: e.dma_start(out=cin_x[0:M, :], in_=S_bf[:, :]), waits=["zseg"], name="zst",
         dma=True, ch="zst")
    P.op("pool", lambda e: e.dma_start(
        out=cin_x[M:M + 1, 0:2 * M].rearrange("o (m b) -> m (o b)", m=M), in_=beta_bits[:, :]),
        waits=["brun3"], name="bst", dma=True, ch="bst")
    P.op("pool", lambda e: e.collective_compute("AllGather", ALU.bypass, replica_groups=[list(range(8))],
                                                ins=[cin_x.opt()], outs=[cout_x.opt()]),
         waits=["zst", "bst"], name="ag_s", coll=True)
    P.op("sp", lambda e: e.dma_start(
        out=beta_sb_bits.ap().rearrange("m (s b) -> m s b", b=2),
        in_=cout_x.rearrange("(s r) c -> s r c", r=M + 2)[:, M, 0:2 * M].rearrange(
            "s (m b) -> m s b", m=M),
        allow_slow_non_contiguous=True),
        waits=["ag_s"], name="bld", dma=True, ch="bld")
    P.op("dve", lambda e: e.tensor_mul(dfac_sb[:, :], beta_sb[:, :], exmask_sb[:, :]),
         waits=["bld", "ld_xm"], name="dfm_raw")
    fdve("dfm", dfac_sb[:, 7:8])
    P.op("act", lambda e: e.activation(dfac_sb[:, :], dfac_sb[:, :], AF.Exp), waits=["dfm"], name="dfac_raw")
    fact("dfac", dfac_sb[:, 7:8])
    for s in range(8):
        wz = ["ag_s"]
        if s >= 2:
            wz += [f"exstep{s - 2}"]
        else:
            wz += ["ch0mm3", "conv_done3"]
        P.op("sp", lambda e, s=s: e.dma_start(out=Zs_bf[s % 2][:, :],
                                              in_=cout_x[s * (M + 2):s * (M + 2) + M, :]),
             waits=wz, name=f"zld{s}", dma=True, ch=f"zld{s % 2}")
        P.op("dve", lambda e, s=s: e.tensor_scalar(T_state[:, :], T_state[:, :], dfac_sb[:, s:s + 1],
                                                   None, ALU.mult),
             waits=["dfac", "ms_T"] + ([f"exstep{s - 1}"] if s > 0 else []), name=f"exd{s}")
        P.op("dve", lambda e, s=s: e.scalar_tensor_tensor(
            T_state[:, :], Zs_bf[s % 2][:, :], exmask_sb[:, s:s + 1], T_state[:, :], ALU.mult, ALU.add),
            waits=[f"zld{s}", f"exd{s}"], name=f"exstep{s}")
    P.op("act", lambda e: e.activation(S_init_bf[:, :], T_state[:, :], AF.Copy),
         waits=["exstep7", "ulast6"], name="sinit")

    def emit_x1t(t):
        pps = [O_ps, A_ps][t % 2]
        for j in range(16):
            wtr = []
            if j == 0:
                wtr = [f"x1b{t}", f"p2acc{t}"]
            P.op("pe", lambda e, j=j, t=t, pps=pps: e.transpose(
                pps[:, j * 128:(j + 1) * 128], y1b[t % 2][:, j * 128:(j + 1) * 128],
                ident_sb[:, :]), waits=wtr, name=(f"xt1_{t}" if j == 15 else None))
        for q4 in range(4):
            we = [f"xt1_{t}"]
            if q4 == 0 and t == 0:
                we += ["qc_d1"]
            P.op("act", lambda e, q4=q4, t=t, pps=pps: e.activation(
                x1_fm[:, 4 * q4:4 * q4 + 4, t * 128:(t + 1) * 128],
                pps[:, q4 * 512:(q4 + 1) * 512].rearrange("p (a n) -> p a n", n=128),
                AF.Copy), waits=we, name=(f"x1fm{t}" if q4 == 3 else None))

    for t in range(NT):
        wrl = [f"y1_spill{t}"]
        wrl += [f"y1_spill{6 + (t % 2)}"]
        if t >= 2:
            wrl += [f"x1sp{t - 2}", f"x1fm{t - 2}"]
        P.op("sp", lambda e, t=t: e.dma_start(out=y1b[t % 2][:], in_=y1_sp[t * 128:(t + 1) * 128, :]),
             waits=wrl, name=f"y1rl{t}", dma=True, ch=f"xld{t % 2}")
        pps = [O_ps, A_ps][t % 2]
        for s4 in range(4):
            wc = []
            if s4 == 0:
                wc = ["sinit"]
                wc += [f"p2acc{t - 2}", f"x1fm{t - 2}"] if t >= 2 else (["y1acc7"] if t == 0 else [])
            P.op("pe", lambda e, s4=s4, t=t, pps=pps: e.matmul(
                pps[:, s4 * 512:(s4 + 1) * 512], rB_store[:, t * 128:(t + 1) * 128],
                S_init_bf[:, s4 * 512:(s4 + 1) * 512], start=True, stop=True),
                waits=wc, name=(f"cor{t}" if s4 == 3 else None))
        if t > 0:
            emit_x1t(t - 1)
        P.op("dve", lambda e, t=t, pps=pps: e.scalar_tensor_tensor(
            y1b[t % 2][:], pps[:, :], 1.0, y1b[t % 2][:], ALU.mult, ALU.add),
            waits=[f"cor{t}", f"y1rl{t}"], name=f"p2acc{t}")
        P.op("dve", lambda e, t=t: e.reduce_sum(sum_parts[:, t, 0:1], y1b[t % 2][:],
                                                axis=mybir.AxisListType.X), name=f"p2rs{t}")
        wsq = [f"p2acc{t}"]
        P.op("dve", lambda e, t=t: e.scalar_tensor_tensor(
            sq_bf[:], y1b[t % 2][:], 1.0, y1b[t % 2][:], ALU.mult, ALU.mult),
            waits=wsq, name=f"p2sq{t}_sq")
        P.op("dve", lambda e, t=t: e.reduce_sum(ssq_parts[:, t, 0:1], sq_bf[:],
                                                axis=mybir.AxisListType.X), name=f"p2sq{t}")
        gapD()
        P.op("dve", lambda e, t=t: e.tensor_scalar(ln8_a[:, t:t + 1], sum_parts[:, t, 0:1],
                                                   1.0 / D, None, ALU.mult),
             waits=[f"p2acc{t}"], name=f"p2m{t}")
        gapD()
        P.op("dve", lambda e, t=t: e.tensor_mul(ln8_b[:, t:t + 1], ln8_a[:, t:t + 1], ln8_a[:, t:t + 1]),
             name=f"p2m2{t}")
        gapD()
        P.op("dve", lambda e, t=t: e.scalar_tensor_tensor(
            ln8_c[:, t:t + 1], ssq_parts[:, t, 0:1], 1.0 / D, ln8_b[:, t:t + 1],
            ALU.mult, ALU.subtract), waits=[f"p2sq{t}", f"p2m2{t}"], name=f"p2var{t}_raw")
        fdve(f"p2var{t}", ln8_c[:, t:t + 1], 128)
        P.op("act", lambda e, t=t: e.activation(ln8_c[:, t:t + 1], ln8_c[:, t:t + 1], AF.Ln,
                                                bias=eps_col[:, :]),
             waits=[f"p2var{t}", "ms_eps"], name=f"p2ln{t}")
        gapA()
        P.op("act", lambda e, t=t: e.activation(ln8_c[:, t:t + 1], ln8_c[:, t:t + 1], AF.Exp, scale=-0.5),
             name=f"p2rstd{t}_raw")
        fact(f"p2rstd{t}", ln8_c[:, t:t + 1], 128)
        P.op("dve", lambda e, t=t: e.tensor_scalar(y1b[t % 2][:], y1b[t % 2][:], ln8_a[:, t:t + 1],
                                                   None, ALU.subtract),
             waits=[f"p2rstd{t}", f"p2sq{t}"], name=f"x1a{t}")
        P.op("dve", lambda e, t=t: e.tensor_scalar(y1b[t % 2][:], y1b[t % 2][:], ln8_c[:, t:t + 1],
                                                   None, ALU.mult), name=f"x1b{t}")
        P.op("pool", lambda e, t=t: e.dma_start(out=x1_sp[t * 128:(t + 1) * 128, :], in_=y1b[t % 2][:]),
             waits=[f"x1b{t}"], name=f"x1sp{t}", dma=True, ch=f"x1sp{t % 2}")
    emit_x1t(NT - 1)

    gbank = [O_ps[:, 0:512], O_ps[:, 1024:1536]]
    ubank = [O_ps[:, 512:1024], O_ps[:, 1536:2048]]

    def emit_ln2_stats(b):
        c4 = slice(4 * b, 4 * b + 4)
        w = [f"y2q{b}_3_{x}" for x in range(4)]
        P.op("dve", lambda e: e.reduce_sum(ln8_a[:, c4], sum_parts[:, c4, :],
                                           axis=mybir.AxisListType.X), waits=w, name=f"fsum{b}")
        gapD()
        P.op("dve", lambda e: e.reduce_sum(ln8_b[:, c4], ssq_parts[:, c4, :],
                                           axis=mybir.AxisListType.X), name=f"fssq{b}")
        P.op("dve", lambda e: e.tensor_scalar(ln8_a[:, c4], ln8_a[:, c4], 1.0 / D, None, ALU.mult),
             waits=[f"fsum{b}"], name=f"fm{b}")
        gapD()
        P.op("dve", lambda e: e.tensor_mul(ln8_c[:, c4], ln8_a[:, c4], ln8_a[:, c4]), name=f"fm2{b}")
        gapD()
        P.op("dve", lambda e: e.scalar_tensor_tensor(
            ln8_b[:, c4], ln8_b[:, c4], 1.0 / D, ln8_c[:, c4], ALU.mult, ALU.subtract),
            waits=[f"fssq{b}", f"fm2{b}"], name=f"fvar{b}_raw")
        fdve(f"fvar{b}")
        P.op("act", lambda e: e.activation(ln8_b[:, c4], ln8_b[:, c4], AF.Ln, bias=eps_col[:, :]),
             waits=[f"fvar{b}"], name=f"fln{b}")
        gapA()
        P.op("act", lambda e: e.activation(ln8_b[:, c4], ln8_b[:, c4], AF.Exp, scale=-0.5),
             name=f"frstd{b}_raw")
        fact(f"frstd{b}")

    def emit_y2r(tg):
        wrl = [f"y2w{tg // 4}_3_{tg % 4}", "umm1_47", "gmm1_47"]
        if tg >= 2:
            wrl += [f"outw{tg - 2}"]
        P.op("act", lambda e, tg=tg: e.dma_start(out=y2rl[tg % 2][:], in_=y2_sp[tg * 128:(tg + 1) * 128, :]),
             waits=wrl, name=f"y2r{tg}", dma=True, ch=f"y2r{tg % 2}")

    def emit_ln2_tile(tg):
        P.op("dve", lambda e, tg=tg: e.tensor_scalar(y2rl[tg % 2][:], y2rl[tg % 2][:],
                                                     ln8_a[:, tg:tg + 1], None, ALU.subtract),
             waits=[f"frstd{tg // 4}", f"y2r{tg}"], name=f"outa{tg}")
        P.op("dve", lambda e, tg=tg: e.tensor_scalar(y2rl[tg % 2][:], y2rl[tg % 2][:],
                                                     ln8_b[:, tg:tg + 1], None, ALU.mult),
             name=f"outb{tg}")
        eng = "act" if tg < 4 else "sp"
        P.op(eng, lambda e, tg=tg: e.dma_start(out=out[tg * 128:(tg + 1) * 128, :], in_=y2rl[tg % 2][:]),
             waits=[f"outb{tg}"], name=f"outw{tg}", dma=True, ch=f"outw{eng}{tg % 2}")

    for h in range(2):
        def emit_gu_load(h, idx):
            r, q = divmod(idx, 3)
            for tag, co, buf in [("g", cout_g, gst), ("u", cout_u, ust)]:
                src = co[r * 128:(r + 1) * 128, :].rearrange("p (a i) -> p a i", i=768)[
                    :, :, q * 256:(q + 1) * 256]
                wld = [f"ag_{tag}"]
                if h == 0 and idx in (0, 1):
                    wld += ["ulast6", "ulast7", "xbf_done6", "xbf_done7"]
                if idx >= 2 or h > 0:
                    pidx, ph = (idx - 2, h) if idx >= 2 else (22 + idx, h - 1)
                    wld += [f"{'gmm' if tag == 'g' else 'umm'}{ph}_{2 * pidx + 1}"]
                if tag == "u" and h == 0 and idx in (0, 1):
                    wld += ["exstep7" if idx == 0 else "sinit"]
                P.op("sp", lambda e, s=src, b=buf, idx=idx: e.dma_start(out=b[idx % 2][:, :, :], in_=s),
                     waits=wld, name=f"{tag}ld{h}_{idx}", dma=True, ch=f"{tag}ld{idx % 2}")

        emit_gu_load(h, 0)
        emit_gu_load(h, 1)
        for qq in range(NIC):
            idx, ic2 = divmod(qq, 2)
            sl48 = (qq + 16) % NIC
            for tag, buf, bank, nm in [("g", gst, gbank[qq % 2], "gmm"), ("u", ust, ubank[qq % 2], "umm")]:
                for a in range(16):
                    wmmq = []
                    if a == 0:
                        wmmq = [f"{tag}ld{h}_{idx}", f"x1fm{4 * h + 3}"]
                        if qq >= 2:
                            wmmq += [f"{'gev' if tag == 'g' else 'uev'}{h}_{qq - 2}"]
                        elif h == 0:
                            wmmq += ["p2acc6", "y1acc7", "x1fm6"]
                        else:
                            wmmq += [f"y2m0_3_{x}" for x in range(4)]
                    P.op("pe", lambda e, a=a, b=buf, idx=idx, ic2=ic2, bank=bank, h=h: e.matmul(
                        bank, b[idx % 2][:, a, ic2 * 128:(ic2 + 1) * 128],
                        x1_fm[:, a, h * 512:(h + 1) * 512], start=(a == 0), stop=(a == 15)),
                        waits=wmmq, name=(f"{nm}{h}_{qq}" if a == 15 else None))
            wge = [f"gmm{h}_{qq}", "ld_gs"]
            if h == 0 and qq == 0:
                wge += ["sresmm7"]
            if h == 0 and qq == 32:
                wge += ["x1sp7", "x1fm7"]
            if h == 1:
                wge += ["dfin0_3"]
            P.op("act", lambda e, qq=qq, sl=sl48, bank=gbank[qq % 2]: e.activation(
                fuse[:, sl, :], bank, AF.Silu, scale=gsc_sb[:, qq:qq + 1]),
                waits=wge, name=f"gev{h}_{qq}")
            P.op("dve", lambda e, qq=qq, bank=ubank[qq % 2]: e.tensor_scalar(
                uscr[:, :], bank, usc_sb[:, qq:qq + 1], None, ALU.mult),
                waits=[f"umm{h}_{qq}", "ld_us"] + ([f"fuse{h}_{qq - 1}"] if qq > 0 else
                                                  ([f"fuse{h - 1}_47"] if h > 0 else
                                                   ["conv_done3", "silu3_15"])),
                name=f"uev{h}_{qq}")
            P.op("dve", lambda e, qq=qq, sl=sl48: e.tensor_mul(fuse[:, sl, :], fuse[:, sl, :], uscr[:, :]),
                 waits=[f"gev{h}_{qq}", f"uev{h}_{qq}"], name=f"fuse{h}_{qq}")
            if qq % 2 == 1 and qq // 2 + 2 < 24:
                emit_gu_load(h, qq // 2 + 2)
        def emit_d_load(h, s, ig):
            wd = ["ag_d"]
            if h == 0 and s == 0 and ig in (0, 1):
                wd += ["rkmm3", "mgmm3"]
            if ig >= 2:
                wd += [f"dcons{h}_{s}_{ig - 2}"]
            elif s > 0:
                wd += [f"dcons{h}_{s - 1}_{10 + ig}"]
            elif h > 0:
                wd += [f"dcons{h - 1}_3_{10 + ig}"]
            P.op("sp", lambda e, s=s, ig=ig: e.dma_start(
                out=dst_b[ig % 2][:, :, :],
                in_=cout_d[ig * 512:(ig + 1) * 512, s * 512:(s + 1) * 512].rearrange(
                    "(c p) n -> p c n", p=128)),
                waits=wd, name=f"dld{h}_{s}_{ig}", dma=True, ch=f"dld{ig % 2}")

        for s in range(4):
            dbank = [A_ps, O_ps][s % 2]
            emit_d_load(h, s, 0)
            for ig in range(12):
                for ic in range(4):
                    q = ig * 4 + ic
                    for tt in range(4):
                        wdm = []
                        if ic == 0 and tt == 0:
                            wdm = [f"dld{h}_{s}_{ig}"]
                        if q == 0 and tt == 0:
                            wdm += [f"fuse{h}_47"]
                            if s == 0:
                                wdm += ["p2acc7", "x1fm7"] if h == 0 else [f"y2m0_2_{x}" for x in range(4)]
                            elif s == 1:
                                wdm += [f"gev{h}_46", f"gev{h}_47", f"uev{h}_46", f"uev{h}_47"]
                            else:
                                wdm += [f"y2m{h}_{s - 2}_{x}" for x in range(4)]
                        nm = None
                        if ic == 3 and tt == 3:
                            nm = f"dcons{h}_{s}_{ig}"
                        P.op("pe", lambda e, q=q, tt=tt, ig=ig, ic=ic, s=s, h=h, dbank=dbank: e.matmul(
                            dbank[:, tt * 512:(tt + 1) * 512], fuse[:, (q + 16) % NIC, tt * 128:(tt + 1) * 128],
                            dst_b[ig % 2][:, ic, :],
                            start=(q == 0), stop=(q == NIC - 1)), waits=wdm, name=nm)
                if ig + 1 < 12:
                    emit_d_load(h, s, ig + 1)
            P.op("dve", lambda e: e.engine_nop(), waits=[f"dcons{h}_{s}_11"], name=f"dfin{h}_{s}")
            for tt in range(4):
                tg = h * 4 + tt
                wx = [f"x1sp{tg}"]
                if h == 0 and s == 0 and tt in (0, 1):
                    wx += ["ch0mm3"]
                if (s * 4 + tt) >= 2:
                    px = (s, tt - 2) if tt >= 2 else (s - 1, tt + 2)
                    wx += [f"y2a{h}_{px[0]}_{px[1]}"]
                elif h > 0:
                    px = (3, tt + 2)
                    wx += [f"y2a{h - 1}_{px[0]}_{px[1]}"]
                P.op("pool", lambda e, s=s, tg=tg: e.dma_start(
                    out=x1rl[tg % 2][:, :], in_=x1_sp[tg * 128:(tg + 1) * 128, s * 512:(s + 1) * 512]),
                    waits=wx, name=f"x1rl{h}_{s}_{tt}", dma=True, ch=f"x1rl{tt % 2}")
                wy2 = [f"dfin{h}_{s}"]
                if h == 0 and s == 0 and tt in (0, 1):
                    wy2 += ["ch1mm3_1_3"]
                if (s * 4 + tt) >= 2 or h > 0:
                    if (s * 4 + tt) >= 2:
                        py = (h, s, tt - 2) if tt >= 2 else (h, s - 1, tt + 2)
                    else:
                        py = (h - 1, 3, tt + 2)
                    wy2 += [f"y2w{py[0]}_{py[1]}_{py[2]}"]
                P.op("dve", lambda e, s=s, tt=tt, dbank=dbank: e.tensor_mul(
                    y2st[tt % 2][:, :], dbank[:, tt * 512:(tt + 1) * 512],
                    dsc_bcast[:, s * 512:(s + 1) * 512]),
                    waits=wy2 + ["ev_dscb3"], name=f"y2m{h}_{s}_{tt}")
                P.op("dve", lambda e, s=s, tt=tt, tg=tg: e.scalar_tensor_tensor(
                    y2st[tt % 2][:, :], x1rl[tg % 2][:, :], 1.0, y2st[tt % 2][:, :], ALU.mult, ALU.add),
                    waits=[f"y2m{h}_{s}_{tt}", f"x1rl{h}_{s}_{tt}"], name=f"y2a{h}_{s}_{tt}")
                P.op("dve", lambda e, s=s, tt=tt, tg=tg: e.reduce_sum(
                    sum_parts[:, tg, s:s + 1], y2st[tt % 2][:, :], axis=mybir.AxisListType.X),
                    name=f"y2rs{h}_{s}_{tt}")
                if tt > 0:
                    pq = (h, s, tt - 1)
                elif s > 0:
                    pq = (h, s - 1, 3)
                elif h > 0:
                    pq = (h - 1, 3, 3)
                else:
                    pq = None
                P.op("act", lambda e, tt=tt: e.activation(
                    sq_bf[:, 0:512], y2st[tt % 2][:, :], AF.Square),
                    waits=[f"y2a{h}_{s}_{tt}"] +
                          ([f"y2q{pq[0]}_{pq[1]}_{pq[2]}"] if pq else ["p2sq7"]),
                    name=f"y2q{h}_{s}_{tt}_sq")
                P.op("dve", lambda e, s=s, tt=tt, tg=tg: e.reduce_sum(
                    ssq_parts[:, tg, s:s + 1], sq_bf[:, 0:512], axis=mybir.AxisListType.X),
                    waits=[f"y2q{h}_{s}_{tt}_sq"], name=f"y2q{h}_{s}_{tt}")
                P.op("pool", lambda e, s=s, tg=tg, tt=tt: e.dma_start(
                    out=y2_sp[tg * 128:(tg + 1) * 128, s * 512:(s + 1) * 512], in_=y2st[tt % 2][:, :]),
                    waits=[f"y2a{h}_{s}_{tt}", f"y2q{h}_{s}_{tt}"], name=f"y2w{h}_{s}_{tt}", dma=True,
                    ch=f"y2w{tt % 2}")
            if h == 1:
                if s == 0:
                    emit_ln2_stats(0)
                    emit_y2r(0)
                    emit_y2r(1)
                elif s == 1:
                    emit_ln2_tile(0)
                    emit_y2r(2)
                elif s == 2:
                    emit_ln2_tile(1)
                    emit_y2r(3)
                else:
                    emit_ln2_tile(2)
                    emit_ln2_tile(3)

    emit_ln2_stats(1)
    emit_y2r(4)
    emit_y2r(5)
    emit_ln2_tile(4)
    emit_y2r(6)
    emit_ln2_tile(5)
    emit_y2r(7)
    emit_ln2_tile(6)
    emit_ln2_tile(7)
    P.op("sp", lambda e: None, waits=[f"outw{tg}" for tg in range(NT)] + ["y2w1_3_3"])

    from contextlib import ExitStack
    with ExitStack() as stk:
        e = stk.enter_context
        sems = {}
        for eng in Prog.ENGINES:
            if P.ccount[eng]:
                sems[("c", eng)] = e(nc.semaphore(f"sc_{eng}"))
        for i, k in enumerate(sorted(P.dch)):
            sems[k] = e(nc.semaphore(f"sd{i}_{k[1]}_{k[2]}"))
        coll_sems = [e(nc.semaphore(f"cc{i}")) for i in range(getattr(P, "ncoll", 0))]
        block = e(nc.Block())

        @block.sync
        def _(sync):
            P.emit("sp", sync, sems, coll_sems)

        @block.gpsimd
        def _(g):
            P.emit("pool", g, sems, coll_sems)

        @block.vector
        def _(v):
            P.emit("dve", v, sems, coll_sems)

        @block.scalar
        def _(s):
            P.emit("act", s, sems, coll_sems)

        @block.tensor
        def _(tens):
            P.emit("pe", tens, sems, coll_sems)

    _CACHE["nc"] = nc
    return nc


def ci_prev(gi, c):
    t = 2 * gi + c - 1
    return f"{t // 2}_{t % 2}"


def kernel(**inputs):
    x = np.ascontiguousarray(np.asarray(inputs["L0_attn_hidden_states"], np.float32))
    gi = lambda k: np.ascontiguousarray(np.asarray(inputs[k], np.float32))
    conv_w = gi("L0_attn_conv_w"); conv_b = gi("L0_attn_conv_b")
    in_proj = gi("L0_attn_in_proj_w"); ch0 = gi("L0_attn_ch_gate_w0")
    ch1 = gi("L0_attn_ch_gate_w1"); ch_b1 = gi("L0_attn_ch_gate_b1")
    mem_w = gi("L0_attn_mem_gate_w"); mem_b = gi("L0_attn_mem_gate_b")
    res_w = gi("L0_attn_residual_weight")
    gate_w = gi("L0_ffn_gate_w"); up_w = gi("L0_ffn_up_w"); down_w = gi("L0_ffn_down_w")
    gsc = gi("L0_ffn_gate_scale"); usc = gi("L0_ffn_up_scale"); dsc = gi("L0_ffn_down_scale")

    def pack_pc(v, k):
        return np.ascontiguousarray(v.reshape(k, 128).T)

    def pack_fm3(arr, inner):
        return np.ascontiguousarray(arr.reshape(16, 128, inner).transpose(1, 0, 2))

    common = dict(
        w_inprojT=np.ascontiguousarray(in_proj.T),
        w_memT=np.ascontiguousarray(mem_w.T),
        w_ch0T=np.ascontiguousarray(ch0.T),
        w_ch1T=np.ascontiguousarray(ch1.T),
        conv_w_fm=np.ascontiguousarray(conv_w.reshape(16, 128, 4).transpose(1, 0, 2).reshape(128, 64)),
        conv_b_fm=pack_pc(conv_b, 16),
        res_w_fm=pack_pc(res_w, 16),
        mem_b_col=np.ascontiguousarray(mem_b[:, None]),
        ch_b1_row=np.ascontiguousarray(ch_b1[None, :]),
        dsc_row=np.ascontiguousarray(dsc[None, :]),
        gsc_fm=pack_pc(gsc, 48),
        usc_fm=pack_pc(usc, 48),
    )
    nc = build_nc()
    in_maps = []
    for c in range(8):
        seq, p = c // 4, c % 4
        xm = np.ascontiguousarray(x[seq, p * 1024:(p + 1) * 1024])
        if p > 0:
            halo = x[seq, p * 1024 - 3:p * 1024]
        else:
            halo = np.zeros((3, 2048), np.float32)
        halo_fm = np.ascontiguousarray(halo.T.reshape(16, 128, 3).transpose(1, 0, 2))
        em = np.zeros((M, 8), np.float32)
        for s in range(8):
            if s // 4 == seq and s % 4 < p:
                em[:, s] = 1.0
        m = dict(common)
        m.update(
            x_main=xm, halo_fm=halo_fm, exmask=em,
            gate_sl=np.ascontiguousarray(gate_w[:, c * 768:(c + 1) * 768]),
            up_sl=np.ascontiguousarray(up_w[:, c * 768:(c + 1) * 768]),
            down_sl=np.ascontiguousarray(down_w[c * 768:(c + 1) * 768, :]),
        )
        in_maps.append(m)
    res = run_bass_kernel_spmd(nc, in_maps, core_ids=list(range(8)))
    outs = [np.asarray(res.results[c]["out"], np.float32) for c in range(8)]
    full = np.stack([np.concatenate(outs[0:4], 0), np.concatenate(outs[4:8], 0)], 0)
    return full
